# version 16
# baseline (speedup 1.0000x reference)
import numpy as np
import ml_dtypes
from contextlib import ExitStack

import concourse.bass as bass
import concourse.mybir as mybir
from concourse.bass_utils import run_bass_kernel_spmd

F32 = mybir.dt.float32
BF16 = mybir.dt.bfloat16
I32 = mybir.dt.int32
I8 = mybir.dt.int8
AF = mybir.ActivationFunctionType
ALU = mybir.AluOpType
AX = mybir.AxisListType

B, S, H = 2, 4096, 4096
NH, QR, KVR, RD, ND, VD = 32, 1536, 512, 64, 128, 128
QK = RD + ND
DFF = 11008
NF = DFF // 128
NT = S // 128
NQT = 8
SCALE = QK ** -0.5
LN2PI = 6.283185307179586
INV2PI = 0.15915494309189535

PHASES = ("SETUP", "A", "B", "C", "D")
TAPS = ()
SERIAL_A1 = False
SERIAL_B2 = False
XQ = "gpsimd"


class _Sem:
    __slots__ = ("h", "n")
    def __init__(self, h):
        self.h, self.n = h, 0
    def bump(self, k=1):
        self.n += k
        return self.h


class _K:
    ENGS = ("tensor", "scalar", "vector", "sync", "gpsimd")

    def __init__(self, nc, es):
        self.nc, self.es = nc, es
        self.q = {e: [] for e in self.ENGS}
        self.taps = {}

    def sem(self, name):
        return _Sem(self.es.enter_context(self.nc.semaphore(name)))

    def emit(self, eng, fn):
        self.q[eng].append(fn)

    def wait(self, eng, sem, val=None):
        v = sem.n if val is None else val
        if v <= 0:
            return
        h = sem.h
        self.q[eng].append(lambda e: e.wait_ge(h, v))

    def run(self, block):
        for eng in self.ENGS:
            fns = self.q[eng]
            def body(e, fns=fns):
                for f in fns:
                    f(e)
            getattr(block, eng)(body)


def _dma(k, eng, dst, src, sem=None):
    if sem is None:
        k.emit(eng, lambda e, d=dst, s=src: e.dma_start(out=d, in_=s))
        return None
    h = sem.bump(16)
    k.emit(eng, lambda e, d=dst, s=src, h=h: e.dma_start(out=d, in_=s).then_inc(h, 16))
    return sem.n


def _mm(k, out, stat, mov, start, stop, sem=None):
    if sem is None:
        k.emit("tensor", lambda e, o=out, a=stat, b=mov, st=start, sp=stop:
               e.matmul(o, a, b, start=st, stop=sp))
        return None
    h = sem.bump(1)
    k.emit("tensor", lambda e, o=out, a=stat, b=mov, st=start, sp=stop, h=h:
           e.matmul(o, a, b, start=st, stop=sp).then_inc(h, 1))
    return sem.n


def _op(k, eng, fn, sem=None):
    if sem is None:
        k.emit(eng, fn)
        return None
    h = sem.bump(1)
    k.emit(eng, lambda e, f=fn, h=h: f(e).then_inc(h, 1))
    return sem.n


def build(nc):
    es = ExitStack()
    k = _K(nc, es)

    d_x = nc.dram_tensor("x", [S, H], F32, kind="ExternalInput").ap()
    d_qbase = nc.dram_tensor("qbase", [1, 1], F32, kind="ExternalInput").ap()
    d_ident = nc.dram_tensor("ident", [128, 128], BF16, kind="ExternalInput").ap()
    d_wkva = nc.dram_tensor("wkva_t", [128, 32, 576], F32, kind="ExternalInput").ap()
    d_gkva = nc.dram_tensor("gkva_t", [128, 4], F32, kind="ExternalInput").ap()
    if "A2" in PHASES or "A" in PHASES:
        d_wqa = nc.dram_tensor("wqa_t", [3, 128, 32 * 512], F32, kind="ExternalInput").ap()
        d_gqa = nc.dram_tensor("gqa_t", [128, 12], F32, kind="ExternalInput").ap()
    if "B" in PHASES:
        d_gate = nc.dram_tensor("gate_t", [NF, 128, 4096], I8, kind="ExternalInput").ap()
        d_up = nc.dram_tensor("up_t", [NF, 128, 4096], I8, kind="ExternalInput").ap()
        d_down = nc.dram_tensor("down_t", [8, 128, NF * 512], I8, kind="ExternalInput").ap()
        d_dns = nc.dram_tensor("dns", [1, H], F32, kind="ExternalInput").ap()
        d_gs = nc.dram_tensor("gs_t", [128, NF], F32, kind="ExternalInput").ap()
        d_us = nc.dram_tensor("us_t", [128, NF], F32, kind="ExternalInput").ap()
    if "C" in PHASES:
        d_wqb = nc.dram_tensor("wqb_t", [128, 32, 12, 192], F32, kind="ExternalInput").ap()
        d_wkvb = nc.dram_tensor("wkvb_t", [128, 32, 4, 256], F32, kind="ExternalInput").ap()
    if "D" in PHASES:
        d_wo = nc.dram_tensor("wo_t", [8, 128, 32, 512], F32, kind="ExternalInput").ap()
    d_y = nc.dram_tensor("y", [1024, H], F32, kind="ExternalOutput").ap()
    d_scr_g = nc.dram_tensor("scr_g", [1024, H], BF16,
                             kind=("ExternalOutput" if "gout" in TAPS else "Internal")).ap()
    d_scr_kc = nc.dram_tensor("scr_kc", [128, 4 * S], BF16, kind="Internal").ap()
    d_scr_kr = nc.dram_tensor("scr_kr", [64, S], BF16, kind="Internal").ap()
    d_scr_qc = nc.dram_tensor("scr_qc", [128, 12 * 1024], BF16, kind="Internal").ap()
    d_scr_h = nc.dram_tensor("scr_h", [128, NF * 512], BF16, kind="Internal").ap()

    s_ldx = k.sem("ldx")
    s_ldw = k.sem("ldw")
    s_ldm = k.sem("ldm")
    s_pool = k.sem("pool")
    s_mm = k.sem("mm")
    s_mm2 = k.sem("mm2")
    s_sc = k.sem("sc")
    s_sc2 = k.sem("sc2")
    s_sc3 = k.sem("sc3")
    s_sc4 = k.sem("sc4")
    s_vc = k.sem("vc")
    s_vc2 = k.sem("vc2")
    s_st = k.sem("st")
    s_gst = k.sem("gst")
    s_xfree = k.sem("xfree")

    def sb(name, shape, dt, stack=None):
        return (stack or es).enter_context(nc.sbuf_tensor(name, shape, dt))

    def psum(name, shape, stack=None, dt=F32):
        return (stack or es).enter_context(nc.psum_tensor(name, shape, dt))

    def tap(name, ap, shape, dtype):
        if name not in TAPS:
            return
        t = nc.dram_tensor("tap_" + name, list(shape), dtype, kind="ExternalOutput").ap()
        _dma(k, "sync", t[:, :], ap, sem=s_st)
        for eng in ("tensor", "vector", "scalar", "gpsimd"):
            k.wait(eng, s_st, s_st.n)
        k.taps[name] = True

    def emit_tables(k, sb, psum, scope, sb_ones, sb_qbase, dst_sin, dst_cos, sems, tag):
        (s_ldm, s_pool, s_mm, s_sc, s_vc) = sems
        t_dp = sb("t_dp" + tag, [64, 1], F32, scope)
        t_dm = sb("t_dm" + tag, [64, 1], F32, scope)
        t_inv = sb("t_inv" + tag, [64, 1], F32, scope)
        t_qb = sb("t_qb" + tag, [64, 1], F32, scope)
        t_pa = sb("t_pa" + tag, [64, S], F32, scope)
        t_pb = sb("t_pb" + tag, [64, S], F32, scope)
        t_pc = sb("t_pc" + tag, [64, S], F32, scope)
        t_pd = sb("t_pd" + tag, [64, S], F32, scope)
        t_ni = sb("t_ni" + tag, [64, S], I32, scope)
        ps_b1 = psum("ps_b1" + tag, [64, 1], scope)

        _op(k, "gpsimd", lambda e: e.iota(t_pd[:, :], pattern=[[1, S]], base=0,
            channel_multiplier=0, allow_small_or_imprecise_dtypes=True), sem=s_pool)
        _op(k, "gpsimd", lambda e: e.iota(t_dp[:, :], pattern=[[0, 1]], base=0,
            channel_multiplier=1, allow_small_or_imprecise_dtypes=True), sem=s_pool)
        ldm_all = s_ldm.n
        pool_all = s_pool.n
        k.wait("vector", s_ldm, ldm_all)
        k.wait("vector", s_pool, pool_all)
        v_ = _op(k, "vector", lambda e: e.tensor_single_scalar(t_dm[:, :], t_dp[:, :], 31.5, ALU.is_ge), sem=s_vc)
        k.wait("vector", s_vc, v_)
        v_ = _op(k, "vector", lambda e: e.tensor_scalar_mul(t_dm[:, :], t_dm[:, :], -32.0), sem=s_vc)
        k.wait("vector", s_vc, v_)
        vc_dm = _op(k, "vector", lambda e: e.tensor_add(t_inv[:, :], t_dm[:, :], t_dp[:, :]), sem=s_vc)
        k.wait("scalar", s_vc, vc_dm)
        sc_inv = _op(k, "scalar", lambda e: e.activation(t_inv[:, :], t_inv[:, :], AF.Exp,
                     scale=-0.28782313662425583), sem=s_sc)
        k.wait("tensor", s_ldm, ldm_all)
        k.wait("tensor", s_pool, pool_all)
        mm_qb = _mm(k, ps_b1[:, :], sb_ones[0:1, 0:64], sb_qbase[0:1, 0:1], True, True, sem=s_mm)
        k.wait("scalar", s_mm, mm_qb)
        sc_qb = _op(k, "scalar", lambda e: e.copy(t_qb[:, :], ps_b1[:, :]), sem=s_sc)
        k.wait("vector", s_sc, sc_qb)
        _op(k, "vector", lambda e: e.tensor_scalar_add(t_pc[:, :], t_pd[:, :], t_qb[:, 0:1]))
        _op(k, "vector", lambda e: e.tensor_single_scalar(t_pb[:, :], t_pc[:, :], 4095.5, ALU.is_ge))
        _op(k, "vector", lambda e: e.tensor_scalar_mul(t_pb[:, :], t_pb[:, :], -4096.0))
        _op(k, "vector", lambda e: e.tensor_add(t_pa[:, :], t_pb[:, :], t_pc[:, :]))
        k.wait("vector", s_sc, sc_inv)
        vc_pa = _op(k, "vector", lambda e: e.tensor_scalar_mul(t_pa[:, :], t_pa[:, :], t_inv[:, 0:1]), sem=s_vc)
        k.wait("vector", s_vc, vc_pa)

        def wrap_sin2(dst, shift):
            if shift:
                _op(k, "vector", lambda e, sh=shift: e.tensor_scalar_add(t_pc[:, :], t_pa[:, :], sh))
            else:
                _op(k, "vector", lambda e: e.tensor_copy(t_pc[:, :], t_pa[:, :]))
            _op(k, "vector", lambda e: e.tensor_scalar_mul(t_pb[:, :], t_pc[:, :], INV2PI))
            _op(k, "vector", lambda e: e.tensor_copy(t_ni[:, :], t_pb[:, :]))
            _op(k, "vector", lambda e: e.tensor_copy(t_pb[:, :], t_ni[:, :]))
            _op(k, "vector", lambda e: e.tensor_scalar_mul(t_pb[:, :], t_pb[:, :], -LN2PI))
            v = _op(k, "vector", lambda e: e.tensor_add(t_pd[:, :], t_pb[:, :], t_pc[:, :]), sem=s_vc)
            k.wait("scalar", s_vc, v)
            return _op(k, "scalar", lambda e, d=dst: e.activation(d, t_pd[:, :], AF.Sin), sem=s_sc)

        wrap_sin2(dst_sin, 0.0)
        return wrap_sin2(dst_cos, np.pi / 2)

    with es:
        sb_ident = sb("ident_sb", [128, 128], BF16)
        sb_ones = sb("ones_sb", [1, 128], F32)
        sb_qbase = sb("qbase_sb", [1, 1], F32)
        p1 = ExitStack(); es.enter_context(p1)
        sb_xnT = [sb("xnT%d" % g, [128, 32 * 512], BF16, p1) for g in range(2)]

        _dma(k, "gpsimd", sb_ident[:, :], d_ident[:, :], sem=s_ldm)
        _dma(k, "gpsimd", sb_qbase[:, :], d_qbase[:, :], sem=s_ldm)
        vc_ones = _op(k, "vector", lambda e: e.memset(sb_ones[:, :], 1.0), sem=s_vc)

        pa = ExitStack(); es.enter_context(pa)
        sb_krT = sb("krT", [64, S], BF16, pa)
        sb_cos = sb("cosT", [64, S], BF16, pa)
        sb_sin = sb("sinT", [64, S], BF16, pa)
        sb_gkv = sb("gkv", [128, 4], F32, pa)
        sb_gqa = sb("gqa_sb", [128, 12], F32, pa)

        if "SETUP" in PHASES:
            st_ = ExitStack(); pa.enter_context(st_)
            t_dp = sb("t_dp", [64, 1], F32, st_)
            t_dm = sb("t_dm", [64, 1], F32, st_)
            t_inv = sb("t_inv", [64, 1], F32, st_)
            t_qb = sb("t_qb", [64, 1], F32, st_)
            t_pa = sb("t_pa", [64, S], F32, st_)
            t_pb = sb("t_pb", [64, S], F32, st_)
            t_pc = sb("t_pc", [64, S], F32, st_)
            t_pd = sb("t_pd", [64, S], F32, st_)
            t_ni = sb("t_ni", [64, S], I32, st_)
            ps_b1 = psum("ps_b1", [64, 1], st_)

            _op(k, "gpsimd", lambda e: e.iota(t_pd[:, :], pattern=[[1, S]], base=0,
                channel_multiplier=0, allow_small_or_imprecise_dtypes=True), sem=s_pool)
            _op(k, "gpsimd", lambda e: e.iota(t_dp[:, :], pattern=[[0, 1]], base=0,
                channel_multiplier=1, allow_small_or_imprecise_dtypes=True), sem=s_pool)
            _dma(k, "gpsimd", sb_gkv[:, :], d_gkva[:, :], sem=s_ldm)
            if "A" in PHASES:
                _dma(k, "gpsimd", sb_gqa[:, :], d_gqa[:, :], sem=s_ldm)
            ldm_all = s_ldm.n
            pool_all = s_pool.n

            k.wait("vector", s_ldm, ldm_all)
            k.wait("vector", s_pool, pool_all)
            v_ = _op(k, "vector", lambda e: e.tensor_single_scalar(t_dm[:, :], t_dp[:, :], 31.5, ALU.is_ge), sem=s_vc)
            k.wait("vector", s_vc, v_)
            v_ = _op(k, "vector", lambda e: e.tensor_scalar_mul(t_dm[:, :], t_dm[:, :], -32.0), sem=s_vc)
            k.wait("vector", s_vc, v_)
            vc_dm = _op(k, "vector", lambda e: e.tensor_add(t_inv[:, :], t_dm[:, :], t_dp[:, :]), sem=s_vc)
            k.wait("scalar", s_vc, vc_dm)
            sc_inv = _op(k, "scalar", lambda e: e.activation(t_inv[:, :], t_inv[:, :], AF.Exp,
                         scale=-0.28782313662425583), sem=s_sc)
            k.wait("tensor", s_ldm, ldm_all)
            k.wait("tensor", s_pool, pool_all)
            k.wait("tensor", s_vc, vc_ones)
            mm_qb = _mm(k, ps_b1[:, :], sb_ones[0:1, 0:64], sb_qbase[0:1, 0:1], True, True, sem=s_mm)
            k.wait("scalar", s_mm, mm_qb)
            sc_qb = _op(k, "scalar", lambda e: e.copy(t_qb[:, :], ps_b1[:, :]), sem=s_sc)
            k.wait("vector", s_sc, sc_qb)
            _op(k, "vector", lambda e: e.tensor_scalar_add(t_pc[:, :], t_pd[:, :], t_qb[:, 0:1]))
            _op(k, "vector", lambda e: e.tensor_single_scalar(t_pb[:, :], t_pc[:, :], 4095.5, ALU.is_ge))
            _op(k, "vector", lambda e: e.tensor_scalar_mul(t_pb[:, :], t_pb[:, :], -4096.0))
            _op(k, "vector", lambda e: e.tensor_add(t_pa[:, :], t_pb[:, :], t_pc[:, :]))
            k.wait("vector", s_sc, sc_inv)
            vc_pa = _op(k, "vector", lambda e: e.tensor_scalar_mul(t_pa[:, :], t_pa[:, :], t_inv[:, 0:1]), sem=s_vc)
            k.wait("vector", s_vc, vc_pa)

            def wrap_sin(dst, shift):
                if shift:
                    _op(k, "vector", lambda e, sh=shift: e.tensor_scalar_add(t_pc[:, :], t_pa[:, :], sh))
                else:
                    _op(k, "vector", lambda e: e.tensor_copy(t_pc[:, :], t_pa[:, :]))
                _op(k, "vector", lambda e: e.tensor_scalar_mul(t_pb[:, :], t_pc[:, :], INV2PI))
                _op(k, "vector", lambda e: e.tensor_copy(t_ni[:, :], t_pb[:, :]))
                _op(k, "vector", lambda e: e.tensor_copy(t_pb[:, :], t_ni[:, :]))
                _op(k, "vector", lambda e: e.tensor_scalar_mul(t_pb[:, :], t_pb[:, :], -LN2PI))
                v = _op(k, "vector", lambda e: e.tensor_add(t_pd[:, :], t_pb[:, :], t_pc[:, :]), sem=s_vc)
                k.wait("scalar", s_vc, v)
                return _op(k, "scalar", lambda e, d=dst: e.activation(d, t_pd[:, :], AF.Sin), sem=s_sc)

            k.wait("sync", s_vc, s_vc.n)
            tap("freqs", t_pa[:, :], [64, S], F32)
            tap("inv", t_inv[:, :], [64, 1], F32)
            tap("qb", t_qb[:, :], [64, 1], F32)
            wrap_sin(sb_sin[:, :], 0.0)
            sc_tabs = wrap_sin(sb_cos[:, :], np.pi / 2)
            k.wait("sync", s_sc, sc_tabs)
            tap("cosT", sb_cos[:, :], [64, S], BF16)
            tap("sinT", sb_sin[:, :], [64, S], BF16)
            k.wait("gpsimd", s_sc, sc_tabs)
            k.wait("tensor", s_sc, sc_tabs)
            k.wait("vector", s_sc, sc_tabs)
            st_.close()

        if "A" in PHASES:
            pa1 = ExitStack(); pa.enter_context(pa1)
            xb = [sb("xb%d" % i, [128, H], F32, pa1) for i in range(3)]
            sb_xnbf = [sb("xnbf%d" % i, [128, H], BF16, pa1) for i in range(2)]
            sb_wkva = sb("wkva_sb", [128, 32 * 576], BF16, pa1)
            t_xt = sb("t_xt", [128, 4096], BF16, pa1)
            t_sum = sb("t_sum", [128, 1], F32, pa1)
            t_ssq = sb("t_ssq", [128, 1], F32, pa1)
            t_mean = sb("t_mean", [128, 1], F32, pa1)
            t_var = sb("t_var", [128, 1], F32, pa1)
            t_m2 = sb("t_m2", [128, 1], F32, pa1)
            t_v1 = sb("t_v1", [128, 1], F32, pa1)
            t_rstd = sb("t_rstd", [128, 1], F32, pa1)
            t_kss = sb("t_kss", [128, 1], F32, pa1)
            t_krc = sb("t_krc", [128, 1], F32, pa1)
            t_kcb = sb("t_kcb", [128, 512], BF16, pa1)
            kcst = [sb("kcst%d" % i, [128, 512], BF16, pa1) for i in range(2)]
            t_krb = sb("t_krb", [128, 64], BF16, pa1)
            t_krr = sb("t_krr", [64, 128], BF16, pa1)
            t_krrot = sb("t_krrot", [64, 128], BF16, pa1)
            t_kr2 = sb("t_kr2", [64, 128], BF16, pa1)
            t_kr3 = sb("t_kr3", [64, 128], BF16, pa1)
            t_junk = sb("t_junk", [128, 1], F32, pa1)
            ps_tr = [psum("ps_tr%d" % i, [128, 512], pa1) for i in range(2)]
            ps_kv = psum("ps_kv", [128, 512], pa1)
            ps_kv2 = psum("ps_kv2", [128, 64], pa1)
            ps_ktr = psum("ps_ktr", [128, 512], pa1)
            ps_krtr = psum("ps_krtr", [64, 128], pa1)

            for cc in range(4):
                _dma(k, "gpsimd", sb_wkva[:, cc * 8 * 576:(cc + 1) * 8 * 576],
                     d_wkva[:, cc * 8:(cc + 1) * 8, :].rearrange("p a b -> p (a b)"), sem=s_ldw)
            ldw_wkva = s_ldw.n

            xfree_v = {}; xfree_s = {}
            kst = {}
            kc_ev = {}
            trgrp = 0
            sc2_evgrp = 0
            trl = {}
            prev = {"vc_kcb": 0, "sc_kcT": 0, "sc_krb": 0,
                    "mm_ktr": 0, "mm_krtr": 0, "vc_krT": 0, "sc_krr": 0, "mm_kv": 0}

            for i in range(NT):
                bi = i % 3
                my = i < NQT
                xnb = sb_xnbf[i % 2]
                if i >= 3:
                    k.wait(XQ, s_vc, xfree_v[i - 3])
                    k.wait(XQ, s_sc, xfree_s[i - 3])
                if SERIAL_A1:
                    k.wait(XQ, s_mm, s_mm.n)
                for q4 in range(8):
                    _dma(k, XQ, xb[bi][:, q4 * 512:(q4 + 1) * 512],
                         d_x[i * 128:(i + 1) * 128, q4 * 512:(q4 + 1) * 512], sem=s_ldx)
                ldx_i = s_ldx.n
                if i >= 1:
                    k.wait(XQ, s_sc3, kc_ev[i - 1])
                    _dma(k, XQ,
                         d_scr_kc[:, :].rearrange("p (a s) -> p a s", s=S)
                         [:, :, (i - 1) * 128:i * 128],
                         kcst[(i - 1) % 2][:, :].rearrange("p (a f) -> p a f", f=128),
                         sem=s_gst)
                    kst[i - 1] = s_gst.n
                k.wait("vector", s_ldx, ldx_i)
                vc_red = _op(k, "vector", lambda e, b=bi: e.reduce_sum(t_sum[:, :], xb[b][:, :], axis=AX.X), sem=s_vc)
                k.wait("vector", s_vc, vc_red)
                k.wait("scalar", s_ldx, ldx_i)
                if i >= 2 and trl[i - 2]:
                    k.wait("scalar", s_mm, trl[i - 2])
                sc_sq = _op(k, "scalar", lambda e, b=bi, xnb=xnb: e.activation(
                    xnb[:, :], xb[b][:, :], AF.Square, accum_out=t_ssq[:, :]), sem=s_sc)
                v_ = _op(k, "vector", lambda e: e.tensor_scalar_mul(t_mean[:, :], t_sum[:, :], 1.0 / H), sem=s_vc)
                k.wait("vector", s_vc, v_)
                v_ = _op(k, "vector", lambda e: e.tensor_mul(t_m2[:, :], t_mean[:, :], t_mean[:, :]), sem=s_vc)
                k.wait("vector", s_vc, v_)
                k.wait("vector", s_sc, sc_sq)
                v_ = _op(k, "vector", lambda e: e.tensor_scalar_mul(t_v1[:, :], t_ssq[:, :], 1.0 / H), sem=s_vc)
                k.wait("vector", s_vc, v_)
                v_ = _op(k, "vector", lambda e: e.tensor_sub(t_var[:, :], t_v1[:, :], t_m2[:, :]), sem=s_vc)
                k.wait("vector", s_vc, v_)
                vc_var = _op(k, "vector", lambda e: e.tensor_scalar_add(t_var[:, :], t_var[:, :], 1e-5), sem=s_vc)
                k.wait("scalar", s_vc, vc_var)
                s_ = _op(k, "scalar", lambda e: e.activation(t_rstd[:, :], t_var[:, :], AF.Ln), sem=s_sc)
                k.wait("scalar", s_sc, s_)
                sc_rstd = _op(k, "scalar", lambda e: e.activation(
                    t_rstd[:, :], t_rstd[:, :], AF.Exp, scale=-0.5), sem=s_sc)
                k.wait("vector", s_sc, sc_rstd)
                if i >= 2 and trl[i - 2]:
                    k.wait("vector", s_mm, trl[i - 2])
                vc_xn = _op(k, "vector", lambda e, b=bi, xnb=xnb: e.tensor_scalar(
                    xnb[:, :], xb[b][:, :], t_mean[:, 0:1], t_rstd[:, 0:1],
                    ALU.subtract, ALU.mult), sem=s_vc)
                if i == 0 and "xn0" in TAPS:
                    k.wait("sync", s_vc, vc_xn)
                    tap("xn0", xnb[:, :], [128, H], BF16)
                    tap("sum0", t_sum[:, :], [128, 1], F32)
                    tap("ssq0", t_ssq[:, :], [128, 1], F32)
                    tap("mean0", t_mean[:, :], [128, 1], F32)
                    tap("var0", t_var[:, :], [128, 1], F32)
                    tap("rstd0", t_rstd[:, :], [128, 1], F32)
                    k.wait("vector", s_st, s_st.n)
                    k.wait("scalar", s_st, s_st.n)
                    k.wait("tensor", s_st, s_st.n)
                xfree_v[i] = vc_xn
                xfree_s[i] = sc_sq
                if SERIAL_A1:
                    k.wait("tensor", s_vc, s_vc.n)
                    k.wait("tensor", s_sc, s_sc.n)
                    k.wait("tensor", s_sc2, s_sc2.n)
                    k.wait("tensor", s_sc3, s_sc3.n)
                    k.wait("tensor", s_sc4, s_sc4.n)
                k.wait("tensor", s_vc, vc_xn)
                for g in range(8):
                    trgrp += 1
                    if trgrp > 2:
                        k.wait("tensor", s_sc2, trgrp - 2)
                    mmg = None
                    for c4 in range(4):
                        c = g * 4 + c4
                        mmg = _mm(k, ps_tr[g % 2][:, c4 * 128:(c4 + 1) * 128],
                                  xnb[:, c * 128:(c + 1) * 128], sb_ident[:, :],
                                  True, True, sem=(s_mm if c4 == 3 else None))
                    k.wait("scalar", s_mm, mmg)
                    if my:
                        qg, qc = (0 if (i % 8) < 4 else 1), (i % 4)
                        _op(k, "scalar", lambda e, qg=qg, qc=qc, g=g: e.copy(
                            sb_xnT[qg][:, :].rearrange("p (c b) -> p c b", b=512)
                                [:, 4 * g:4 * g + 4, qc * 128:(qc + 1) * 128],
                            ps_tr[g % 2][:, :].rearrange("p (a f) -> p a f", f=128)),
                            sem=s_sc2)
                    else:
                        _op(k, "scalar", lambda e, g=g: e.copy(
                            t_xt[:, g * 512:(g + 1) * 512], ps_tr[g % 2][:, :]), sem=s_sc2)
                    sc2_evgrp += 1
                trl[i] = s_mm.n
                if my:
                    qg, qc = (0 if (i % 8) < 4 else 1), (i % 4)
                    last_dst = sb_xnT[qg][:, 31 * 512 + qc * 128 + 127: 31 * 512 + qc * 128 + 128]
                else:
                    last_dst = t_xt[:, 4095:4096]
                sc_settle = _op(k, "scalar", lambda e, d=last_dst: e.copy(t_junk[:, :], d), sem=s_sc)
                k.wait("tensor", s_ldw, ldw_wkva)
                k.wait("tensor", s_sc2, sc2_evgrp)
                k.wait("tensor", s_sc, sc_settle)
                if prev["vc_kcb"]:
                    k.wait("tensor", s_vc, prev["vc_kcb"])
                if prev["sc_krb"]:
                    k.wait("tensor", s_sc, prev["sc_krb"])
                mm_kv = mm_kv2 = None
                for c in range(32):
                    if my:
                        qg, qc = (0 if (i % 8) < 4 else 1), (i % 4)
                        stat = sb_xnT[qg][:, c * 512 + qc * 128: c * 512 + (qc + 1) * 128]
                    else:
                        stat = t_xt[:, c * 128:(c + 1) * 128]
                    mm_kv = _mm(k, ps_kv[:, :], stat, sb_wkva[:, c * 576: c * 576 + 512],
                                c == 0, c == 31, sem=(s_mm if c == 31 else None))
                    mm_kv2 = _mm(k, ps_kv2[:, :], stat, sb_wkva[:, c * 576 + 512:(c + 1) * 576],
                                 c == 0, c == 31, sem=(s_mm2 if c == 31 else None))
                prev["mm_kv"] = mm_kv
                k.wait("scalar", s_mm, mm_kv)
                sc_acc = _op(k, "scalar", lambda e: e.activation(t_xt[:, 0:512], ps_kv[:, :], AF.Square,
                                                        accum_out=t_kss[:, :]), sem=s_sc)
                k.wait("scalar", s_sc, sc_acc)
                s_ = _op(k, "scalar", lambda e: e.activation(t_kss[:, :], t_kss[:, :], AF.Ln), sem=s_sc)
                k.wait("scalar", s_sc, s_)
                sc_nrm = _op(k, "scalar", lambda e: e.activation(
                    t_kss[:, :], t_kss[:, :], AF.Exp, scale=0.5), sem=s_sc)
                k.wait("vector", s_sc, sc_nrm)
                v_ = _op(k, "vector", lambda e: e.tensor_scalar_add(t_kss[:, :], t_kss[:, :], 2.2627416997969522e-05), sem=s_vc)
                k.wait("vector", s_vc, v_)
                vc_rcp = _op(k, "vector", lambda e: e.reciprocal(t_krc[:, :], t_kss[:, :]), sem=s_vc)
                k.wait("vector", s_vc, vc_rcp)
                if prev["mm_ktr"]:
                    k.wait("vector", s_mm, prev["mm_ktr"])
                _op(k, "vector", lambda e: e.tensor_scalar(t_kcb[:, :], ps_kv[:, :],
                    t_krc[:, 0:1], 22.627416997969522, ALU.mult, ALU.mult))
                vc_kcb = _op(k, "vector", lambda e: e.tensor_copy(t_junk[:, :], t_kcb[:, 511:512]), sem=s_vc)
                prev["vc_kcb"] = vc_kcb
                k.wait("tensor", s_vc, vc_kcb)
                if prev["sc_kcT"]:
                    k.wait("tensor", s_sc3, prev["sc_kcT"])
                mm_ktr = None
                for cc in range(4):
                    mm_ktr = _mm(k, ps_ktr[:, cc * 128:(cc + 1) * 128],
                                 t_kcb[:, cc * 128:(cc + 1) * 128], sb_ident[:, :],
                                 True, True, sem=(s_mm if cc == 3 else None))
                prev["mm_ktr"] = mm_ktr
                k.wait("scalar", s_mm, mm_ktr)
                if i >= 2:
                    k.wait("scalar", s_gst, kst[i - 2])
                sc_kcT = None
                for cc in range(4):
                    sc_kcT = _op(k, "scalar", lambda e, cc=cc, i=i: e.activation(
                        kcst[i % 2][:, cc * 128:(cc + 1) * 128],
                        ps_ktr[:, cc * 128:(cc + 1) * 128], AF.Copy,
                        scale=sb_gkv[:, cc:cc + 1]), sem=(s_sc3 if cc == 3 else None))
                prev["sc_kcT"] = sc_kcT
                kc_ev[i] = sc_kcT
                k.wait("scalar", s_mm2, mm_kv2)
                if prev["mm_krtr"]:
                    k.wait("scalar", s_mm, prev["mm_krtr"])
                _op(k, "scalar", lambda e: e.copy(t_krb[:, :], ps_kv2[:, :]))
                sc_krb = _op(k, "scalar", lambda e: e.copy(t_junk[:, :], t_krb[:, 63:64]), sem=s_sc)
                prev["sc_krb"] = sc_krb
                k.wait("tensor", s_sc, sc_krb)
                if prev["sc_krr"]:
                    k.wait("tensor", s_sc4, prev["sc_krr"])
                mm_krtr = _mm(k, ps_krtr[:, :], t_krb[:, :], sb_ident[:, :], True, True, sem=s_mm)
                prev["mm_krtr"] = mm_krtr
                k.wait("scalar", s_mm, mm_krtr)
                if prev["vc_krT"]:
                    k.wait("scalar", s_vc, prev["vc_krT"])
                sc_krr = _op(k, "scalar", lambda e: e.copy(t_krr[:, :], ps_krtr[:, :]), sem=s_sc4)
                prev["sc_krr"] = sc_krr
                k.wait("vector", s_sc4, sc_krr)
                _op(k, "vector", lambda e: e.tensor_scalar_mul(t_krrot[0:32, :], t_krr[32:64, :], -1.0))
                _op(k, "vector", lambda e: e.tensor_copy(t_krrot[32:64, :], t_krr[0:32, :]))
                _op(k, "vector", lambda e, i=i: e.tensor_mul(t_kr2[:, :], t_krr[:, :], sb_cos[:, i * 128:(i + 1) * 128]))
                _op(k, "vector", lambda e, i=i: e.tensor_mul(t_kr3[:, :], t_krrot[:, :], sb_sin[:, i * 128:(i + 1) * 128]))
                vc_krT = _op(k, "vector", lambda e, i=i: e.tensor_add(
                    sb_krT[:, i * 128:(i + 1) * 128], t_kr2[:, :], t_kr3[:, :]), sem=s_vc)
                prev["vc_krT"] = vc_krT

            k.wait("sync", s_vc, s_vc.n)
            k.wait("sync", s_sc2, s_sc2.n)
            k.wait("sync", s_sc3, s_sc3.n)
            k.wait("sync", s_sc4, s_sc4.n)
            k.wait(XQ, s_sc3, kc_ev[NT - 1])
            _dma(k, XQ,
                 d_scr_kc[:, :].rearrange("p (a s) -> p a s", s=S)
                 [:, :, (NT - 1) * 128:NT * 128],
                 kcst[(NT - 1) % 2][:, :].rearrange("p (a f) -> p a f", f=128),
                 sem=s_gst)
            kst[NT - 1] = s_gst.n
            k.wait(XQ, s_gst, kst[NT - 1])
            k.wait("sync", s_sc, s_sc.n)
            tap("xn31", sb_xnbf[1][:, :], [128, H], BF16)
            tap("txt31", t_xt[:, :], [128, 4096], BF16)
            tap("xnT0", sb_xnT[0][:, :], [128, 32 * 512], BF16)
            tap("xnT1", sb_xnT[1][:, :], [128, 32 * 512], BF16)
            tap("krT", sb_krT[:, :], [64, S], BF16)
            pa1.close()

        if "A" in PHASES:
            pa2 = ExitStack(); pa.enter_context(pa2)
            sb_qc = sb("qc_bf", [128, 8 * 1536], BF16, pa2)
            sb_qcT = sb("qcT", [128, 12 * 1024], BF16, pa2)
            wqab = [sb("wqab%d" % i, [128, 4 * 512], BF16, pa2) for i in range(4)]
            t_qss = sb("t_qss", [128, 8], F32, pa2)
            t_qs3 = sb("t_qs3", [128, 24], F32, pa2)
            t_qp = sb("t_qp", [128, 1], F32, pa2)
            t_qd = sb("t_qd", [128, 512], BF16, pa2)
            t_qn = sb("t_qn", [128, 1], F32, pa2)
            t_qr = sb("t_qr", [128, 8], F32, pa2)
            pq = ExitStack(); pa2.enter_context(pq)
            ps_q = [psum("ps_q%d" % i, [128, 512], pq) for i in range(8)]

            ldw_base = s_ldw.n
            def wqa_load(n):
                t, cg = n // 8, n % 8
                for hh2 in range(2):
                    _dma(k, "gpsimd", wqab[n % 4][:, hh2 * 1024:(hh2 + 1) * 1024],
                         d_wqa[t, :, cg * 2048 + hh2 * 1024: cg * 2048 + (hh2 + 1) * 1024],
                         sem=s_ldw)
            k.wait("gpsimd", s_sc4, s_sc4.n)
            k.wait("gpsimd", s_vc, s_vc.n)
            k.wait("tensor", s_sc, s_sc.n)
            k.wait("tensor", s_sc3, s_sc3.n)
            k.wait("tensor", s_sc4, s_sc4.n)
            k.wait("tensor", s_vc, s_vc.n)
            for n0 in range(4):
                wqa_load(n0)
            grp_mm = {}
            prev_evq = 0
            for t3 in range(3):
                mm_q = None
                for cg in range(8):
                    n = t3 * 8 + cg
                    k.wait("tensor", s_ldw, ldw_base + 32 * min(n + 2, 24))
                    for c4 in range(4):
                        c = cg * 4 + c4
                        if t3 > 0 and cg == 0 and c4 == 0:
                            k.wait("tensor", s_sc3, prev_evq)
                        for qt in range(8):
                            qg, qc = qt // 4, qt % 4
                            stat = sb_xnT[qg][:, c * 512 + qc * 128: c * 512 + (qc + 1) * 128]
                            last = (c4 == 3 and qt == 7)
                            mm_q = _mm(k, ps_q[qt][:, :], stat,
                                       wqab[n % 4][:, c4 * 512:(c4 + 1) * 512],
                                       c == 0, c == 31,
                                       sem=(s_mm2 if last else None))
                    grp_mm[n] = s_mm2.n
                    if n + 4 <= 23:
                        k.wait("gpsimd", s_mm2, grp_mm[n])
                        wqa_load(n + 4)
                k.wait("scalar", s_mm2, grp_mm[t3 * 8 + 7])
                sc_ev = None
                for qt in range(8):
                    sc_acc = _op(k, "scalar", lambda e, qt=qt: e.activation(
                        t_qd[:, :], ps_q[qt][:, :], AF.Square, accum_out=t_qp[:, :]), sem=s_sc)
                    k.wait("scalar", s_sc, sc_acc)
                    k.wait("vector", s_sc, sc_acc)
                    vq = _op(k, "vector", lambda e, qt=qt, t3=t3: e.tensor_copy(
                        t_qs3[:, qt * 3 + t3: qt * 3 + t3 + 1], t_qp[:, :]), sem=s_vc)
                    k.wait("vector", s_vc, vq)
                    k.wait("scalar", s_vc, vq)
                    sc_ev = _op(k, "scalar", lambda e, qt=qt, t3=t3: e.copy(
                        sb_qc[:, qt * 1536 + t3 * 512: qt * 1536 + (t3 + 1) * 512],
                        ps_q[qt][:, :]), sem=s_sc3)
                prev_evq = sc_ev
            pq.close()
            k.wait("sync", s_sc3, s_sc3.n)
            k.wait("sync", s_vc, s_vc.n)
            tap("qcraw", sb_qc[:, :], [128, 8 * 1536], BF16)

            pq2 = ExitStack(); pa2.enter_context(pq2)
            ps_qtr = [psum("ps_qtr%d" % i, [128, 512], pq2) for i in range(2)]
            t_diag = sb("t_diag", [128, 8 * 128], BF16, pq2)
            k.wait("tensor", s_sc3, prev_evq)
            k.wait("vector", s_sc3, prev_evq)
            k.wait("scalar", s_vc, s_vc.n)
            for qt in range(8):
                v_ = _op(k, "vector", lambda e, qt=qt: e.tensor_add(
                    t_qss[:, qt:qt + 1], t_qs3[:, qt * 3: qt * 3 + 1], t_qs3[:, qt * 3 + 1: qt * 3 + 2]), sem=s_vc)
                k.wait("vector", s_vc, v_)
                v_ = _op(k, "vector", lambda e, qt=qt: e.tensor_add(
                    t_qn[:, :], t_qss[:, qt:qt + 1], t_qs3[:, qt * 3 + 2: qt * 3 + 3]), sem=s_vc)
                k.wait("scalar", s_vc, v_)
                s_ = _op(k, "scalar", lambda e: e.activation(t_qn[:, :], t_qn[:, :], AF.Ln), sem=s_sc)
                k.wait("scalar", s_sc, s_)
                s_ = _op(k, "scalar", lambda e: e.activation(t_qn[:, :], t_qn[:, :], AF.Exp, scale=0.5), sem=s_sc)
                k.wait("vector", s_sc, s_)
                v_ = _op(k, "vector", lambda e: e.tensor_scalar_add(t_qn[:, :], t_qn[:, :], 3.9191835884530846e-05), sem=s_vc)
                k.wait("vector", s_vc, v_)
                v_ = _op(k, "vector", lambda e, qt=qt: e.reciprocal(t_qr[:, qt:qt + 1], t_qn[:, :]), sem=s_vc)
                k.wait("vector", s_vc, v_)
                _op(k, "vector", lambda e, qt=qt: e.tensor_scalar(
                    t_diag[:, qt * 128:(qt + 1) * 128], sb_ident[:, :],
                    t_qr[:, qt:qt + 1], 39.191835884530846, ALU.mult, ALU.mult))
            vset = _op(k, "vector", lambda e: e.tensor_copy(t_qp[:, :], t_diag[:, 1023:1024]), sem=s_vc)
            k.wait("sync", s_vc, vset)
            tap("diag", t_diag[:, :], [128, 1024], BF16)
            k.wait("tensor", s_vc, vset)
            ntr = 0
            prev_ev2 = {}
            dmm = {}
            for qt in range(8):
                for g in range(3):
                    ntr += 1
                    if ntr > 2:
                        k.wait("tensor", s_sc3, prev_ev2[ntr - 2])
                    mmg = None
                    for c4 in range(4):
                        c = g * 4 + c4
                        mmg = _mm(k, ps_qtr[ntr % 2][:, c4 * 128:(c4 + 1) * 128],
                                  sb_qc[:, qt * 1536 + c * 128: qt * 1536 + (c + 1) * 128],
                                  t_diag[:, qt * 128:(qt + 1) * 128], True, True, sem=(s_mm2 if c4 == 3 else None))
                    k.wait("scalar", s_mm2, mmg)
                    ev = None
                    for c4 in range(4):
                        c = g * 4 + c4
                        ev = _op(k, "scalar", lambda e, c=c, qt=qt, c4=c4, ntr=ntr: e.activation(
                            sb_qcT[:, c * 1024 + qt * 128: c * 1024 + (qt + 1) * 128],
                            ps_qtr[ntr % 2][:, c4 * 128:(c4 + 1) * 128], AF.Copy,
                            scale=sb_gqa[:, c:c + 1]), sem=(s_sc3 if c4 == 3 else None))
                    prev_ev2[ntr] = ev
                dmm[qt] = s_mm2.n
            pq2.close()
            k.wait("sync", s_sc3, s_sc3.n)
            tap("gqa", sb_gqa[:, :], [128, 12], F32)
            tap("qcT", sb_qcT[:, :], [128, 12 * 1024], BF16)

            if "C" in PHASES or "SPILL" in PHASES:
                k.wait("gpsimd", s_sc3, s_sc3.n)
                k.wait("gpsimd", s_vc, s_vc.n)
                _dma(k, "gpsimd", d_scr_kr[:, :], sb_krT[:, :], sem=s_gst)
                _dma(k, "gpsimd", d_scr_qc[:, :], sb_qcT[:, :], sem=s_gst)
                k.wait("gpsimd", s_gst, s_gst.n)

        if "B" in PHASES:
            pa.close()
            k.wait("gpsimd", s_sc3, s_sc3.n)
            k.wait("gpsimd", s_vc, s_vc.n)
            k.wait("gpsimd", s_mm2, s_mm2.n)
            pb = ExitStack(); es.enter_context(pb)
            sb_gsv = sb("gs_sb", [128, NF], F32, pb)
            sb_usv = sb("us_sb", [128, NF], F32, pb)
            sb_sd = sb("sd_bc", [128, H], BF16, pb)
            sb_hT = sb("hT", [128, NF * 512], BF16, pb)
            t_sg = sb("t_sg", [128, 512], F32, pb)
            t_su = sb("t_su", [128, 512], F32, pb)
            g_bf = [sb("g_bf%d" % i, [128, 512], BF16, pb) for i in range(2)]
            t_jk = sb("t_jkb", [128, 4], BF16, pb)

            _dma(k, "gpsimd", sb_gsv[:, :], d_gs[:, :], sem=s_ldm)
            _dma(k, "gpsimd", sb_usv[:, :], d_us[:, :], sem=s_ldm)
            ldm_b = s_ldm.n

            pbc = ExitStack(); pb.enter_context(pbc)
            ps_bc = psum("ps_bc2", [128, 512], pbc)
            for j in range(8):
                ldj = _dma(k, "gpsimd", t_su[0:1, :], d_dns[:, j * 512:(j + 1) * 512], sem=s_ldm)
                k.wait("tensor", s_ldm, ldj)
                if j > 0:
                    k.wait("tensor", s_sc, sc_bc)
                mmj = _mm(k, ps_bc[:, :], sb_ones[0:1, :], t_su[0:1, :], True, True, sem=s_mm)
                k.wait("scalar", s_mm, mmj)
                sc_bc = _op(k, "scalar", lambda e, j=j: e.copy(sb_sd[:, j * 512:(j + 1) * 512], ps_bc[:, :]), sem=s_sc)
                k.wait("gpsimd", s_sc, sc_bc)
            pbc.close()

            pbg = ExitStack(); pb.enter_context(pbg)
            gbuf = [sb("gbuf%d" % i, [128, 4096], BF16, pbg) for i in range(2)]
            ubuf = [sb("ubuf%d" % i, [128, 4096], BF16, pbg) for i in range(2)]
            pb1 = ExitStack(); pb.enter_context(pb1)
            ps_g = [psum("ps_g%d" % i, [128, 512], pb1) for i in range(4)]
            ps_u = [psum("ps_u%d" % i, [128, 512], pb1) for i in range(4)]
            h1b = [sb("h1b%d" % i, [128, 512], BF16, pb1) for i in range(2)]
            ld0 = s_ldw.n
            fmm = {}
            fsg = {}
            fsu = {}
            hst = {0: 0}
            def b1_load(f):
                _dma(k, "gpsimd", gbuf[f % 2][:, :], d_gate[f], sem=s_ldw)
                _dma(k, "gpsimd", ubuf[f % 2][:, :], d_up[f], sem=s_ldw)
            b1_load(0)
            b1_load(1)
            for f in range(NF):
                k.wait("tensor", s_ldw, ld0 + 32 * (f + 1 if f > 0 else 2))
                if f >= 2:
                    k.wait("tensor", s_vc, fsu[f - 2])
                mmf = None
                for qg in range(2):
                    for c in range(32):
                        _mm(k, ps_g[qg * 2 + f % 2][:, :], gbuf[f % 2][:, c * 128:(c + 1) * 128],
                            sb_xnT[qg][:, c * 512:(c + 1) * 512], c == 0, c == 31)
                    for c in range(32):
                        mmf = _mm(k, ps_u[qg * 2 + f % 2][:, :], ubuf[f % 2][:, c * 128:(c + 1) * 128],
                                  sb_xnT[qg][:, c * 512:(c + 1) * 512], c == 0, c == 31,
                                  sem=(s_mm if (qg == 1 and c == 31) else None))
                fmm[f] = mmf
                if f >= 1:
                    k.wait("gpsimd", s_vc, fsu[f - 1])
                    _dma(k, "gpsimd", d_scr_h[:, (f - 1) * 512: f * 512], h1b[(f - 1) % 2][:, :], sem=s_gst)
                    hst[f - 1] = s_gst.n
                if f + 2 < NF:
                    k.wait("gpsimd", s_mm, fmm[f])
                    b1_load(f + 2)
                k.wait("scalar", s_mm, fmm[f])
                if f >= 1:
                    k.wait("scalar", s_vc, fsg[f - 1])
                sg1 = _op(k, "scalar", lambda e, f=f: e.activation(
                    t_sg[:, :], ps_g[f % 2][:, :], AF.Silu, scale=sb_gsv[:, f:f + 1]), sem=s_sc)
                k.wait("vector", s_mm, fmm[f])
                k.wait("vector", s_sc, sg1)
                _op(k, "vector", lambda e, f=f: e.tensor_scalar_mul(
                    t_su[:, :], ps_u[f % 2][:, :], sb_usv[:, f:f + 1]))
                vh0 = _op(k, "vector", lambda e, f=f: e.tensor_mul(
                    sb_hT[:, f * 512:(f + 1) * 512], t_su[:, :], t_sg[:, :]), sem=s_vc)
                k.wait("scalar", s_vc, vh0)
                sg2 = _op(k, "scalar", lambda e, f=f: e.activation(
                    t_sg[:, :], ps_g[2 + f % 2][:, :], AF.Silu, scale=sb_gsv[:, f:f + 1]), sem=s_sc)
                k.wait("vector", s_sc, sg2)
                if f >= 2:
                    k.wait("vector", s_gst, hst[f - 2] if (f - 2) in hst else 0)
                _op(k, "vector", lambda e, f=f: e.tensor_scalar_mul(
                    t_su[:, :], ps_u[2 + f % 2][:, :], sb_usv[:, f:f + 1]))
                fsu[f] = _op(k, "vector", lambda e, f=f: e.tensor_mul(
                    h1b[f % 2][:, :], t_su[:, :], t_sg[:, :]), sem=s_vc)
                fsg[f] = fsu[f]
                if f == NF - 1:
                    k.wait("gpsimd", s_vc, fsu[f])
                    _dma(k, "gpsimd", d_scr_h[:, f * 512:(f + 1) * 512], h1b[f % 2][:, :], sem=s_gst)
                    hst[f] = s_gst.n
            pb1.close()
            pbg.close()
            pbd = ExitStack(); pb.enter_context(pbd)
            FCH = 8
            NDB = 4
            dbuf = [sb("dbuf%d" % i, [128, FCH * 512], BF16, pbd) for i in range(NDB)]

            for qg in range(2):
                if qg == 1:
                    k.wait("gpsimd", s_mm2, s_mm2.n)
                    hrel = _dma(k, "gpsimd", sb_hT[:, :], d_scr_h[:, :], sem=s_ldw)
                    k.wait("tensor", s_ldw, s_ldw.n)
                pb2 = ExitStack(); pb.enter_context(pb2)
                ps_d = [psum("ps_d%d_%d" % (qg, i), [128, 512], pb2) for i in range(8)]
                k.wait("tensor", s_vc, fsu[NF - 1])
                ld0 = s_ldw.n
                nfb = (NF + FCH - 1) // FCH
                njt = 8 * nfb
                def b2_load(j):
                    dg2, fb = j // nfb, j % nfb
                    nf = min(FCH, NF - fb * FCH)
                    _dma(k, "gpsimd", dbuf[j % NDB][:, 0:nf * 512],
                         d_down[dg2, :, fb * FCH * 512:(fb * FCH + nf) * 512], sem=s_ldw)
                dval = {}
                mmj = {}
                if not SERIAL_B2:
                    for j0 in range(NDB):
                        b2_load(j0)
                gst_base = s_gst.n
                evd = {}
                for dg2 in range(8):
                    pset = (dg2 % 2) * 4
                    for fb in range(nfb):
                        j = dg2 * nfb + fb
                        if SERIAL_B2:
                            k.wait("gpsimd", s_mm2, s_mm2.n)
                            b2_load(j)
                            nwait = j + 1
                        else:
                            nwait = j + 2
                        k.wait("tensor", s_ldw,
                               ld0 + min(nwait, njt) * 16 + (16 if nwait > njt else 0))
                        nf = min(FCH, NF - fb * FCH)
                        mmfb = None
                        for f4 in range(nf):
                            f = fb * FCH + f4
                            for qt2 in range(4):
                                if f == 0 and qt2 == 0 and dg2 >= 2:
                                    k.wait("tensor", s_vc, evd[dg2 - 2])
                                mmfb = _mm(k, ps_d[pset + qt2][:, :],
                                    sb_hT[:, f * 512 + qt2 * 128: f * 512 + (qt2 + 1) * 128],
                                    dbuf[j % NDB][:, f4 * 512:(f4 + 1) * 512],
                                    f == 0, f == NF - 1,
                                    sem=(s_mm2 if (f4 == nf - 1 and qt2 == 3) else None))
                        mmj[j] = mmfb
                        if not SERIAL_B2:
                            nxt = j + NDB
                            if nxt <= njt - 1:
                                k.wait("gpsimd", s_mm2, mmj[nxt - NDB])
                                b2_load(nxt)
                            elif nxt == njt:
                                _dma(k, "gpsimd", t_jk[:, :], d_down[0, :, 0:4], sem=s_ldw)
                    dval[dg2] = s_mm2.n
                    k.wait("vector", s_mm2, dval[dg2])
                    ev = None
                    for qt2 in range(4):
                        v_ = _op(k, "vector", lambda e, qt2=qt2, pset=pset, dg2=dg2: e.tensor_mul(
                            t_su[:, :], ps_d[pset + qt2][:, :], sb_sd[:, dg2 * 512:(dg2 + 1) * 512]), sem=s_vc)
                        k.wait("scalar", s_vc, v_)
                        if dg2 * 4 + qt2 >= 2:
                            k.wait("scalar", s_gst, gst_base + 16 * (dg2 * 4 + qt2 - 1))
                        ev = _op(k, "scalar", lambda e, qt2=qt2: e.activation(
                            g_bf[qt2 % 2][:, :], t_su[:, :], AF.Silu), sem=s_sc)
                        k.wait("gpsimd", s_sc, ev)
                        _dma(k, "gpsimd", d_scr_g[qg * 512 + qt2 * 128: qg * 512 + (qt2 + 1) * 128,
                                                  dg2 * 512:(dg2 + 1) * 512], g_bf[qt2 % 2][:, :], sem=s_gst)
                        k.wait("vector", s_sc, ev)
                    evd[dg2] = s_vc.n
                pb2.close()
            k.wait("gpsimd", s_gst, s_gst.n)
            pb.close()

        if "C" in PHASES:
            if "B" in PHASES:
                pb.close()
            p1.close()
            k.wait("gpsimd", s_mm2, s_mm2.n)
            k.wait("gpsimd", s_sc, s_sc.n)
            k.wait("gpsimd", s_vc, s_vc.n)
            pc = ExitStack(); es.enter_context(pc)
            sb_aoT = sb("aoT", [128, 32 * 1024], BF16, pc)
            csub = ExitStack(); pc.enter_context(csub)
            sb_cos2 = sb("cosT2", [64, S], BF16, csub)
            sb_sin2 = sb("sinT2", [64, S], BF16, csub)
            st2 = ExitStack(); csub.enter_context(st2)
            sc_tab2 = emit_tables(k, sb, psum, st2, sb_ones, sb_qbase,
                                  sb_sin2[:, :], sb_cos2[:, :],
                                  (s_ldm, s_pool, s_mm, s_sc, s_vc), "_c")
            k.wait("gpsimd", s_sc, sc_tab2)
            k.wait("tensor", s_sc, sc_tab2)
            k.wait("vector", s_sc, sc_tab2)
            st2.close()

            sb_kcT2 = sb("kcT2", [128, 4 * S], BF16, csub)
            sb_krT2 = sb("krT2", [128, S], BF16, csub)
            sb_qcT2 = sb("qcT2", [128, 12 * 1024], BF16, csub)
            wqbb = [sb("wqbb%d" % i, [128, 12 * 192 + 64], BF16, csub) for i in range(2)]
            wkvbb = [sb("wkvbb%d" % i, [128, 4 * 256], BF16, csub) for i in range(2)]
            sb_knT = sb("knT", [128, S], BF16, csub)
            sb_vaug = sb("vaug", [128, 32 * 128], BF16, csub)
            sb_exp = sb("expT", [128, 4 * 512], BF16, csub)
            sb_qnT = sb("qnT", [128, 1024], BF16, csub)
            sb_qrT = sb("qrT", [128, 1024], BF16, csub)
            t_qro = sb("t_qro", [64, 1024], BF16, csub)
            t_qrb = sb("t_qrb", [64, 1024], BF16, csub)
            t_qr2 = sb("t_qr2c", [64, 1024], BF16, csub)
            t_qr3 = sb("t_qr3c", [64, 1024], BF16, csub)
            sb_one2 = sb("one2", [128, 128], BF16, csub)
            t_bc = sb("t_bc", [128, 512], F32, csub)

            _dma(k, "gpsimd", sb_kcT2[:, :], d_scr_kc[:, :], sem=s_ldw)
            _dma(k, "gpsimd", sb_krT2[0:64, :], d_scr_kr[:, :], sem=s_ldw)
            _dma(k, "gpsimd", sb_qcT2[:, :], d_scr_qc[:, :], sem=s_ldw)
            _op(k, "vector", lambda e: e.memset(sb_one2[:, :], 1.0))
            _op(k, "vector", lambda e: e.memset(sb_krT2[64:128, :], 0.0))
            _op(k, "vector", lambda e: e.memset(sb_qrT[64:128, :], 0.0))
            v_ones = _op(k, "vector", lambda e: e.memset(
                wqbb[0][:, 2304:2368], 0.0), sem=s_vc)
            _op(k, "vector", lambda e: e.memset(wqbb[1][:, 2304:2368], 0.0), sem=s_vc)

            def c_load(h):
                _dma(k, "gpsimd", wqbb[h % 2][:, 0:2304].rearrange("p (a b) -> p a b", a=12),
                     d_wqb[:, h], sem=s_ldw)
                _dma(k, "gpsimd", wkvbb[h % 2][:, :].rearrange("p (a b) -> p a b", a=4),
                     d_wkvb[:, h], sem=s_ldw)
            ldc0 = s_ldw.n
            c_load(0)
            c_load(1)

            ppc = ExitStack(); csub.enter_context(ppc)
            pb_ = [psum("pbank%d" % i, [128, 512], ppc) for i in range(8)]

            hp_qn = {}; hp_qr = {}; hp_wfree = {}
            fin_sc = {0: 0}; fin_vc = {0: 0}
            last_mmb = [0]
            exp_hist = []
            att_hist = []

            for h in range(NH):
                k.wait("tensor", s_ldw, ldc0 + 32 * (h + 1 if h > 0 else 2))
                if h == 0:
                    k.wait("tensor", s_vc, v_ones)
                k.wait("tensor", s_sc, fin_sc[h])
                k.wait("tensor", s_vc, fin_vc[h])
                mmq = None
                for qg2 in range(2):
                    for c in range(12):
                        _mm(k, pb_[qg2][:, :], wqbb[h % 2][:, c * 192: c * 192 + 128],
                            sb_qcT2[:, c * 1024 + qg2 * 512: c * 1024 + (qg2 + 1) * 512],
                            c == 0, c == 11)
                        mmq = _mm(k, pb_[2 + qg2][:, :], wqbb[h % 2][:, c * 192 + 128: c * 192 + 256],
                                  sb_qcT2[:, c * 1024 + qg2 * 512: c * 1024 + (qg2 + 1) * 512],
                                  c == 0, c == 11,
                                  sem=(s_mm if (qg2 == 1 and c == 11) else None))
                k.wait("scalar", s_mm, mmq)
                qn = None
                for qg2 in range(2):
                    qn = _op(k, "scalar", lambda e, qg2=qg2: e.copy(
                        sb_qnT[:, qg2 * 512:(qg2 + 1) * 512], pb_[qg2][:, :]),
                        sem=(s_sc if qg2 == 1 else None))
                hp_qn[h] = qn
                k.wait("vector", s_mm, mmq)
                for qg2 in range(2):
                    _op(k, "vector", lambda e, qg2=qg2: e.tensor_scalar_mul(
                        t_qro[0:32, qg2 * 512:(qg2 + 1) * 512], pb_[2 + qg2][32:64, :], -1.0))
                    _op(k, "vector", lambda e, qg2=qg2: e.tensor_copy(
                        t_qro[32:64, qg2 * 512:(qg2 + 1) * 512], pb_[2 + qg2][0:32, :]))
                    _op(k, "vector", lambda e, qg2=qg2: e.tensor_copy(
                        t_qrb[:, qg2 * 512:(qg2 + 1) * 512], pb_[2 + qg2][0:64, :]))
                _op(k, "vector", lambda e: e.tensor_mul(t_qr2[:, :], t_qrb[:, :], sb_cos2[:, 0:1024]))
                _op(k, "vector", lambda e: e.tensor_mul(t_qr3[:, :], t_qro[:, :], sb_sin2[:, 0:1024]))
                qr = _op(k, "vector", lambda e: e.tensor_add(sb_qrT[0:64, :], t_qr2[:, :], t_qr3[:, :]), sem=s_vc)
                hp_qr[h] = qr
                mmk_ev = {}
                for kg in range(8):
                    if kg >= 2:
                        k.wait("tensor", s_sc2, mmk_ev[kg - 2])
                    mmk = None
                    for cc in range(4):
                        mmk = _mm(k, pb_[6 + kg % 2][:, :], wkvbb[h % 2][:, cc * 256: cc * 256 + 128],
                                  sb_kcT2[:, cc * S + kg * 512: cc * S + (kg + 1) * 512],
                                  cc == 0, cc == 3, sem=(s_mm if cc == 3 else None))
                    k.wait("scalar", s_mm, mmk)
                    mmk_ev[kg] = _op(k, "scalar", lambda e, kg=kg: e.copy(
                        sb_knT[:, kg * 512:(kg + 1) * 512], pb_[6 + kg % 2][:, :]), sem=s_sc2)
                knT_done = mmk_ev[7]
                k.wait("tensor", s_sc, hp_qn[h])
                vev = {}
                for kt4 in range(8):
                    if kt4 >= 2:
                        k.wait("tensor", s_sc3, vev[kt4 - 2])
                    mmv = None
                    for j4 in range(4):
                        kt = kt4 * 4 + j4
                        for cc in range(4):
                            mmv = _mm(k, pb_[kt4 % 2][:, j4 * 128:(j4 + 1) * 128],
                                      sb_kcT2[:, cc * S + kt * 128: cc * S + (kt + 1) * 128],
                                      wkvbb[h % 2][:, cc * 256 + 128:(cc + 1) * 256],
                                      cc == 0, cc == 3,
                                      sem=(s_mm if (j4 == 3 and cc == 3) else None))
                    k.wait("scalar", s_mm, mmv)
                    ev = _op(k, "scalar", lambda e, kt4=kt4: e.copy(
                        sb_vaug[:, kt4 * 512:(kt4 + 1) * 512], pb_[kt4 % 2][:, :]),
                        sem=s_sc3)
                    vev[kt4] = ev
                hp_wfree[h] = s_mm.n
                if h + 2 < NH:
                    k.wait("gpsimd", s_mm, hp_wfree[h])
                    c_load(h + 2)
                vaug_done = vev[7]
                k.wait("tensor", s_sc2, knT_done)
                k.wait("tensor", s_sc3, vaug_done)
                k.wait("tensor", s_vc, hp_qr[h])
                locfin = [fin_sc[h], fin_vc[h]]
                for qg2 in range(2):
                    exp_sc = {}
                    att_mm = {}
                    def emit_attnv(ktl, qg2=qg2, locfin=locfin):
                        if ktl == 0:
                            k.wait("tensor", s_sc, locfin[0])
                            k.wait("tensor", s_vc, locfin[1])
                        k.wait("tensor", s_sc, exp_sc[ktl])
                        _mm(k, pb_[2 + qg2][:, :],
                            sb_vaug[:, ktl * 128:(ktl + 1) * 128],
                            sb_exp[:, (ktl % 4) * 512:((ktl % 4) + 1) * 512],
                            ktl == 0, ktl == NT - 1)
                        mma = _mm(k, pb_[4 + qg2][:, :],
                                  sb_one2[:, :],
                                  sb_exp[:, (ktl % 4) * 512:((ktl % 4) + 1) * 512],
                                  ktl == 0, ktl == NT - 1, sem=s_mm2)
                        att_mm[ktl] = mma
                        att_hist.append(mma)
                    for kt in range(NT):
                        gk = len(exp_hist)
                        if len(exp_hist) >= 2:
                            k.wait("tensor", s_sc, exp_hist[-2])
                        _mm(k, pb_[kt % 2][:, :], sb_knT[:, kt * 128:(kt + 1) * 128],
                            sb_qnT[:, qg2 * 512:(qg2 + 1) * 512], True, False)
                        mms = _mm(k, pb_[kt % 2][:, :], sb_krT2[:, kt * 128:(kt + 1) * 128],
                                  sb_qrT[0:128, qg2 * 512:(qg2 + 1) * 512], False, True, sem=s_mm)
                        k.wait("scalar", s_mm, mms)
                        if len(att_hist) >= 2:
                            k.wait("scalar", s_mm2, att_hist[-2])
                        exp_sc[kt] = _op(k, "scalar", lambda e, kt=kt: e.activation(
                            sb_exp[:, (kt % 4) * 512:((kt % 4) + 1) * 512],
                            pb_[kt % 2][:, :], AF.Exp, scale=SCALE), sem=s_sc)
                        exp_hist.append(exp_sc[kt])
                        if kt >= 2:
                            emit_attnv(kt - 2)
                    emit_attnv(NT - 2)
                    emit_attnv(NT - 1)
                    k.wait("vector", s_mm2, att_mm[NT - 1])
                    k.wait("vector", s_vc, locfin[1])
                    v_ = _op(k, "vector", lambda e, qg2=qg2: e.reciprocal(
                        t_bc[:, :], pb_[4 + qg2][:, :]), sem=s_vc)
                    k.wait("vector", s_vc, v_)
                    fv = _op(k, "vector", lambda e, h=h, qg2=qg2: e.tensor_mul(
                        sb_aoT[:, h * 1024 + qg2 * 512: h * 1024 + (qg2 + 1) * 512],
                        pb_[2 + qg2][:, :], t_bc[:, :]), sem=s_vc)
                    locfin[0] = s_sc.n
                    locfin[1] = s_vc.n
                    if qg2 == 1:
                        fin_sc[h + 1] = s_sc.n
                        fin_vc[h + 1] = s_vc.n
            ppc.close()
            k.wait("gpsimd", s_sc, s_sc.n)
            k.wait("gpsimd", s_vc, s_vc.n)
            csub.close()

            if "D" in PHASES:
                pd_ = ExitStack(); pc.enter_context(pd_)
                wob = [sb("wob%d" % i, [128, 32 * 512], BF16, pd_) for i in range(2)]
                xbf = [sb("xbf%d" % i, [128, 512], F32, pd_) for i in range(4)]
                gbf2 = [sb("gbf2%d" % i, [128, 512], BF16, pd_) for i in range(4)]
                obf = [sb("obf%d" % i, [128, 512], F32, pd_) for i in range(2)]
                obm = sb("obm", [128, 512], F32, pd_)
                ps_w = [psum("ps_w%d" % i, [128, 512], pd_) for i in range(2)]

                ldd0 = s_ldw.n
                def d_wo_load(dg):
                    _dma(k, "gpsimd", wob[dg % 2][:, :].rearrange("p (a b) -> p a b", a=32),
                         d_wo[dg], sem=s_ldw)
                d_wo_load(0)
                d_wo_load(1)
                k.wait("tensor", s_vc, fin_vc[NH])
                ldx0 = s_ldx.n
                def d_xg_load(j, qt, dg):
                    _dma(k, "gpsimd", xbf[j % 4][:, :],
                         d_x[qt * 128:(qt + 1) * 128, dg * 512:(dg + 1) * 512], sem=s_ldx)
                    _dma(k, "gpsimd", gbf2[j % 4][:, :],
                         d_scr_g[qt * 128:(qt + 1) * 128, dg * 512:(dg + 1) * 512], sem=s_ldx)
                for j0 in range(4):
                    d_xg_load(j0, j0, 0)
                jglob = [4]
                wo_mm = {}
                ev_vc = {}
                gst_y0 = s_gst.n
                for dg in range(8):
                    k.wait("tensor", s_ldw, ldd0 + 16 * (dg + 1 if dg > 0 else 2))
                    for qt in range(8):
                        j = dg * 8 + qt
                        if j >= 2:
                            k.wait("tensor", s_vc, ev_vc[j - 2])
                        mmw = None
                        for hh in range(32):
                            mmw = _mm(k, ps_w[j % 2][:, :],
                                      sb_aoT[:, hh * 1024 + qt * 128: hh * 1024 + (qt + 1) * 128],
                                      wob[dg % 2][:, hh * 512:(hh + 1) * 512],
                                      hh == 0, hh == 31, sem=(s_mm if hh == 31 else None))
                        wo_mm[j] = mmw
                        k.wait("vector", s_mm, mmw)
                        k.wait("vector", s_ldx, ldx0 + 32 * (j + 1))
                        _op(k, "vector", lambda e, j=j: e.tensor_mul(
                            obm[:, :], ps_w[j % 2][:, :], gbf2[j % 4][:, :]))
                        if j >= 2:
                            k.wait("vector", s_gst, gst_y0 + 16 * (j - 1))
                        ev_vc[j] = _op(k, "vector", lambda e, j=j: e.tensor_add(
                            obf[j % 2][:, :], obm[:, :], xbf[j % 4][:, :]), sem=s_vc)
                        k.wait("gpsimd", s_vc, ev_vc[j])
                        _dma(k, "gpsimd", d_y[qt * 128:(qt + 1) * 128, dg * 512:(dg + 1) * 512],
                             obf[j % 2][:, :], sem=s_gst)
                        if jglob[0] < 64:
                            nj = jglob[0]
                            nqt, ndg = nj % 8, nj // 8
                            if nj - 4 >= 0 and (nj - 4) in ev_vc:
                                k.wait("gpsimd", s_vc, ev_vc[nj - 4])
                            d_xg_load(nj, nqt, ndg)
                            jglob[0] += 1
                    if dg + 2 < 8:
                        k.wait("gpsimd", s_mm, wo_mm[dg * 8 + 7])
                        d_wo_load(dg + 2)
                k.wait("gpsimd", s_gst, s_gst.n)
            pc.close()

        k.wait("sync", s_st, s_st.n)
        k.wait("gpsimd", s_gst, s_gst.n)

        k.run(es.enter_context(nc.Block()))
    return k


def _i8view(a):
    assert a.dtype == np.int32
    return np.ascontiguousarray(a.view(np.int8).reshape(*a.shape, 4)[..., 0])


def _host_prep(inputs):
    f32 = np.float32
    w_qa = np.ascontiguousarray(inputs["L0_attn_w_qa"], dtype=f32)
    w_qb = np.ascontiguousarray(inputs["L0_attn_w_qb"], dtype=f32)
    w_kva = np.ascontiguousarray(inputs["L0_attn_w_kva"], dtype=f32)
    w_kvb = np.ascontiguousarray(inputs["L0_attn_w_kvb"], dtype=f32)
    w_o = np.ascontiguousarray(inputs["L0_attn_w_o"], dtype=f32)
    gate_w = np.ascontiguousarray(inputs["L0_ffn_gate_w"], dtype=np.int32)
    up_w = np.ascontiguousarray(inputs["L0_ffn_up_w"], dtype=np.int32)
    down_w = np.ascontiguousarray(inputs["L0_ffn_down_w"], dtype=np.int32)
    shared = {
        "ident": np.eye(128, dtype=ml_dtypes.bfloat16),
        "wqa_t": np.ascontiguousarray(w_qa.reshape(32, 128, 3, 512).transpose(2, 1, 0, 3)).reshape(3, 128, 32 * 512),
        "wkva_t": np.ascontiguousarray(w_kva.reshape(32, 128, 576).transpose(1, 0, 2)),
        "wqb_t": np.ascontiguousarray(w_qb.reshape(12, 128, 32, 192).transpose(1, 2, 0, 3)),
        "wkvb_t": np.ascontiguousarray(w_kvb.reshape(4, 128, 32, 256).transpose(1, 2, 0, 3)),
        "gate_t": _i8view(np.ascontiguousarray(gate_w.reshape(NF, 128, 32, 128).transpose(0, 3, 2, 1)).reshape(NF, 128, 4096)),
        "up_t": _i8view(np.ascontiguousarray(up_w.reshape(NF, 128, 32, 128).transpose(0, 3, 2, 1)).reshape(NF, 128, 4096)),
        "down_t": _i8view(np.ascontiguousarray(down_w.reshape(8, 512, NF, 128).transpose(0, 3, 2, 1)).reshape(8, 128, NF * 512)),
        "wo_t": np.ascontiguousarray(w_o.reshape(32, 128, 8, 512).transpose(2, 1, 0, 3)),
        "gqa_t": np.ascontiguousarray(np.asarray(inputs["L0_attn_g_qa"], dtype=f32).reshape(12, 128).T),
        "gkva_t": np.ascontiguousarray(np.asarray(inputs["L0_attn_g_kva"], dtype=f32).reshape(4, 128).T),
        "dns": np.ascontiguousarray(np.asarray(inputs["L0_ffn_down_s"], dtype=f32).reshape(1, H)),
        "gs_t": np.ascontiguousarray(np.asarray(inputs["L0_ffn_gate_s"], dtype=f32).reshape(NF, 128).T),
        "us_t": np.ascontiguousarray(np.asarray(inputs["L0_ffn_up_s"], dtype=f32).reshape(NF, 128).T),
    }
    x = inputs["L0_attn_hidden_states"]
    in_maps = []
    for c in range(8):
        beta, sig = c // 4, c % 4
        m = dict(shared)
        m["x"] = np.ascontiguousarray(np.roll(np.asarray(x[beta], dtype=f32), -sig * 1024, axis=0))
        m["qbase"] = np.array([[sig * 1024.0]], dtype=f32)
        in_maps.append(m)
    return in_maps


def kernel(**inputs):
    in_maps = _host_prep(inputs)
    nc = bass.Bass("TRN2", target_bir_lowering=False, debug=False)
    build(nc)
    res = run_bass_kernel_spmd(nc, in_maps, core_ids=list(range(8)))
    out = np.empty((B, S, H), dtype=np.float32)
    for c in range(8):
        beta, sig = c // 4, c % 4
        out[beta, sig * 1024:(sig + 1) * 1024] = res.results[c]["y"]
    return out


# version 19
# speedup vs baseline: 1.1791x; 1.1791x over previous
import numpy as np
import ml_dtypes
from contextlib import ExitStack

import concourse.bass as bass
import concourse.mybir as mybir
from concourse.bass_utils import run_bass_kernel_spmd

F32 = mybir.dt.float32
BF16 = mybir.dt.bfloat16
I32 = mybir.dt.int32
I8 = mybir.dt.int8
AF = mybir.ActivationFunctionType
ALU = mybir.AluOpType
AX = mybir.AxisListType

B, S, H = 2, 4096, 4096
NH, QR, KVR, RD, ND, VD = 32, 1536, 512, 64, 128, 128
QK = RD + ND
DFF = 11008
NF = DFF // 128
NT = S // 128
NQT = 8
SCALE = QK ** -0.5
LN2PI = 6.283185307179586
INV2PI = 0.15915494309189535

PHASES = ("SETUP", "A", "B", "C", "D")
TAPS = ()
SERIAL_A1 = False
SERIAL_B2 = False
XQ = "gpsimd"


class _Sem:
    __slots__ = ("h", "n")
    def __init__(self, h):
        self.h, self.n = h, 0
    def bump(self, k=1):
        self.n += k
        return self.h


class _K:
    ENGS = ("tensor", "scalar", "vector", "sync", "gpsimd")

    def __init__(self, nc, es):
        self.nc, self.es = nc, es
        self.q = {e: [] for e in self.ENGS}
        self.taps = {}

    def sem(self, name):
        return _Sem(self.es.enter_context(self.nc.semaphore(name)))

    def emit(self, eng, fn):
        self.q[eng].append(fn)

    def wait(self, eng, sem, val=None):
        v = sem.n if val is None else val
        if v <= 0:
            return
        h = sem.h
        self.q[eng].append(lambda e: e.wait_ge(h, v))

    def run(self, block):
        for eng in self.ENGS:
            fns = self.q[eng]
            def body(e, fns=fns):
                for f in fns:
                    f(e)
            getattr(block, eng)(body)


def _dma(k, eng, dst, src, sem=None):
    if sem is None:
        k.emit(eng, lambda e, d=dst, s=src: e.dma_start(out=d, in_=s))
        return None
    h = sem.bump(16)
    k.emit(eng, lambda e, d=dst, s=src, h=h: e.dma_start(out=d, in_=s).then_inc(h, 16))
    return sem.n


def _mm(k, out, stat, mov, start, stop, sem=None):
    if sem is None:
        k.emit("tensor", lambda e, o=out, a=stat, b=mov, st=start, sp=stop:
               e.matmul(o, a, b, start=st, stop=sp))
        return None
    h = sem.bump(1)
    k.emit("tensor", lambda e, o=out, a=stat, b=mov, st=start, sp=stop, h=h:
           e.matmul(o, a, b, start=st, stop=sp).then_inc(h, 1))
    return sem.n


def _op(k, eng, fn, sem=None):
    if sem is None:
        k.emit(eng, fn)
        return None
    h = sem.bump(1)
    k.emit(eng, lambda e, f=fn, h=h: f(e).then_inc(h, 1))
    return sem.n


def build(nc):
    es = ExitStack()
    k = _K(nc, es)

    d_x = nc.dram_tensor("x", [S, H], F32, kind="ExternalInput").ap()
    d_qbase = nc.dram_tensor("qbase", [1, 1], F32, kind="ExternalInput").ap()
    d_ident = nc.dram_tensor("ident", [128, 128], BF16, kind="ExternalInput").ap()
    d_wkva = nc.dram_tensor("wkva_t", [128, 32, 576], F32, kind="ExternalInput").ap()
    d_gkva = nc.dram_tensor("gkva_t", [128, 4], F32, kind="ExternalInput").ap()
    if "A2" in PHASES or "A" in PHASES:
        d_wqa = nc.dram_tensor("wqa_t", [3, 128, 32 * 512], F32, kind="ExternalInput").ap()
        d_gqa = nc.dram_tensor("gqa_t", [128, 12], F32, kind="ExternalInput").ap()
    if "B" in PHASES:
        d_gate = nc.dram_tensor("gate_t", [NF, 128, 4096], I8, kind="ExternalInput").ap()
        d_up = nc.dram_tensor("up_t", [NF, 128, 4096], I8, kind="ExternalInput").ap()
        d_down = nc.dram_tensor("down_t", [8, 128, NF * 512], I8, kind="ExternalInput").ap()
        d_dns = nc.dram_tensor("dns", [1, H], F32, kind="ExternalInput").ap()
        d_gs = nc.dram_tensor("gs_t", [128, NF], F32, kind="ExternalInput").ap()
        d_us = nc.dram_tensor("us_t", [128, NF], F32, kind="ExternalInput").ap()
    if "C" in PHASES:
        d_wqb = nc.dram_tensor("wqb_t", [128, 32, 12, 192], F32, kind="ExternalInput").ap()
        d_wkvb = nc.dram_tensor("wkvb_t", [128, 32, 4, 256], F32, kind="ExternalInput").ap()
    if "D" in PHASES:
        d_wo = nc.dram_tensor("wo_t", [8, 128, 32, 512], F32, kind="ExternalInput").ap()
    d_y = nc.dram_tensor("y", [1024, H], F32, kind="ExternalOutput").ap()
    d_scr_g = nc.dram_tensor("scr_g", [1024, H], BF16,
                             kind=("ExternalOutput" if "gout" in TAPS else "Internal")).ap()
    d_scr_kc = nc.dram_tensor("scr_kc", [128, 4 * S], BF16, kind="Internal").ap()
    d_scr_kr = nc.dram_tensor("scr_kr", [64, S], BF16, kind="Internal").ap()
    d_scr_qc = nc.dram_tensor("scr_qc", [128, 12 * 1024], BF16, kind="Internal").ap()
    d_scr_h = nc.dram_tensor("scr_h", [128, NF * 512], BF16, kind="Internal").ap()
    d_scr_tb = nc.dram_tensor("scr_tb", [64, 2 * S], BF16, kind="Internal").ap()

    s_ldx = k.sem("ldx")
    s_ldw = k.sem("ldw")
    s_ldm = k.sem("ldm")
    s_pool = k.sem("pool")
    s_mm = k.sem("mm")
    s_mm2 = k.sem("mm2")
    s_sc = k.sem("sc")
    s_sc2 = k.sem("sc2")
    s_sc3 = k.sem("sc3")
    s_sc4 = k.sem("sc4")
    s_vc = k.sem("vc")
    s_vc2 = k.sem("vc2")
    s_st = k.sem("st")
    s_gst = k.sem("gst")
    s_xfree = k.sem("xfree")

    def sb(name, shape, dt, stack=None):
        return (stack or es).enter_context(nc.sbuf_tensor(name, shape, dt))

    def psum(name, shape, stack=None, dt=F32):
        return (stack or es).enter_context(nc.psum_tensor(name, shape, dt))

    def tap(name, ap, shape, dtype):
        if name not in TAPS:
            return
        t = nc.dram_tensor("tap_" + name, list(shape), dtype, kind="ExternalOutput").ap()
        _dma(k, "sync", t[:, :], ap, sem=s_st)
        for eng in ("tensor", "vector", "scalar", "gpsimd"):
            k.wait(eng, s_st, s_st.n)
        k.taps[name] = True

    def emit_tables(k, sb, psum, scope, sb_ones, sb_qbase, dst_sin, dst_cos, sems, tag):
        (s_ldm, s_pool, s_mm, s_sc, s_vc) = sems
        t_dp = sb("t_dp" + tag, [64, 1], F32, scope)
        t_dm = sb("t_dm" + tag, [64, 1], F32, scope)
        t_inv = sb("t_inv" + tag, [64, 1], F32, scope)
        t_qb = sb("t_qb" + tag, [64, 1], F32, scope)
        t_pa = sb("t_pa" + tag, [64, S], F32, scope)
        t_pb = sb("t_pb" + tag, [64, S], F32, scope)
        t_pc = sb("t_pc" + tag, [64, S], F32, scope)
        t_pd = sb("t_pd" + tag, [64, S], F32, scope)
        t_ni = sb("t_ni" + tag, [64, S], I32, scope)
        ps_b1 = psum("ps_b1" + tag, [64, 1], scope)

        _op(k, "gpsimd", lambda e: e.iota(t_pd[:, :], pattern=[[1, S]], base=0,
            channel_multiplier=0, allow_small_or_imprecise_dtypes=True), sem=s_pool)
        _op(k, "gpsimd", lambda e: e.iota(t_dp[:, :], pattern=[[0, 1]], base=0,
            channel_multiplier=1, allow_small_or_imprecise_dtypes=True), sem=s_pool)
        ldm_all = s_ldm.n
        pool_all = s_pool.n
        k.wait("vector", s_ldm, ldm_all)
        k.wait("vector", s_pool, pool_all)
        v_ = _op(k, "vector", lambda e: e.tensor_single_scalar(t_dm[:, :], t_dp[:, :], 31.5, ALU.is_ge), sem=s_vc)
        k.wait("vector", s_vc, v_)
        v_ = _op(k, "vector", lambda e: e.tensor_scalar_mul(t_dm[:, :], t_dm[:, :], -32.0), sem=s_vc)
        k.wait("vector", s_vc, v_)
        vc_dm = _op(k, "vector", lambda e: e.tensor_add(t_inv[:, :], t_dm[:, :], t_dp[:, :]), sem=s_vc)
        k.wait("scalar", s_vc, vc_dm)
        sc_inv = _op(k, "scalar", lambda e: e.activation(t_inv[:, :], t_inv[:, :], AF.Exp,
                     scale=-0.28782313662425583), sem=s_sc)
        k.wait("tensor", s_ldm, ldm_all)
        k.wait("tensor", s_pool, pool_all)
        mm_qb = _mm(k, ps_b1[:, :], sb_ones[0:1, 0:64], sb_qbase[0:1, 0:1], True, True, sem=s_mm)
        k.wait("scalar", s_mm, mm_qb)
        sc_qb = _op(k, "scalar", lambda e: e.copy(t_qb[:, :], ps_b1[:, :]), sem=s_sc)
        k.wait("vector", s_sc, sc_qb)
        _op(k, "vector", lambda e: e.tensor_scalar_add(t_pc[:, :], t_pd[:, :], t_qb[:, 0:1]))
        _op(k, "vector", lambda e: e.tensor_single_scalar(t_pb[:, :], t_pc[:, :], 4095.5, ALU.is_ge))
        _op(k, "vector", lambda e: e.tensor_scalar_mul(t_pb[:, :], t_pb[:, :], -4096.0))
        _op(k, "vector", lambda e: e.tensor_add(t_pa[:, :], t_pb[:, :], t_pc[:, :]))
        k.wait("vector", s_sc, sc_inv)
        vc_pa = _op(k, "vector", lambda e: e.tensor_scalar_mul(t_pa[:, :], t_pa[:, :], t_inv[:, 0:1]), sem=s_vc)
        k.wait("vector", s_vc, vc_pa)

        def wrap_sin2(dst, shift):
            if shift:
                _op(k, "vector", lambda e, sh=shift: e.tensor_scalar_add(t_pc[:, :], t_pa[:, :], sh))
            else:
                _op(k, "vector", lambda e: e.tensor_copy(t_pc[:, :], t_pa[:, :]))
            _op(k, "vector", lambda e: e.tensor_scalar_mul(t_pb[:, :], t_pc[:, :], INV2PI))
            _op(k, "vector", lambda e: e.tensor_copy(t_ni[:, :], t_pb[:, :]))
            _op(k, "vector", lambda e: e.tensor_copy(t_pb[:, :], t_ni[:, :]))
            _op(k, "vector", lambda e: e.tensor_scalar_mul(t_pb[:, :], t_pb[:, :], -LN2PI))
            v = _op(k, "vector", lambda e: e.tensor_add(t_pd[:, :], t_pb[:, :], t_pc[:, :]), sem=s_vc)
            k.wait("scalar", s_vc, v)
            return _op(k, "scalar", lambda e, d=dst: e.activation(d, t_pd[:, :], AF.Sin), sem=s_sc)

        wrap_sin2(dst_sin, 0.0)
        return wrap_sin2(dst_cos, np.pi / 2)

    with es:
        sb_ident = sb("ident_sb", [128, 128], BF16)
        sb_ones = sb("ones_sb", [1, 128], F32)
        sb_qbase = sb("qbase_sb", [1, 1], F32)
        p1 = ExitStack(); es.enter_context(p1)
        sb_xnT = [sb("xnT%d" % g, [128, 32 * 512], BF16, p1) for g in range(2)]

        _dma(k, "gpsimd", sb_ident[:, :], d_ident[:, :], sem=s_ldm)
        _dma(k, "gpsimd", sb_qbase[:, :], d_qbase[:, :], sem=s_ldm)
        vc_ones = _op(k, "vector", lambda e: e.memset(sb_ones[:, :], 1.0), sem=s_vc)

        pa = ExitStack(); es.enter_context(pa)
        sb_krT = sb("krT", [64, S], BF16, pa)
        sb_cos = sb("cosT", [64, S], BF16, pa)
        sb_sin = sb("sinT", [64, S], BF16, pa)
        sb_gkv = sb("gkv", [128, 4], F32, pa)
        sb_gqa = sb("gqa_sb", [128, 12], F32, pa)

        if "SETUP" in PHASES:
            st_ = ExitStack(); pa.enter_context(st_)
            t_dp = sb("t_dp", [64, 1], F32, st_)
            t_dm = sb("t_dm", [64, 1], F32, st_)
            t_inv = sb("t_inv", [64, 1], F32, st_)
            t_qb = sb("t_qb", [64, 1], F32, st_)
            t_pa = sb("t_pa", [64, S], F32, st_)
            t_pb = sb("t_pb", [64, S], F32, st_)
            t_pc = sb("t_pc", [64, S], F32, st_)
            t_pd = sb("t_pd", [64, S], F32, st_)
            t_ni = sb("t_ni", [64, S], I32, st_)
            ps_b1 = psum("ps_b1", [64, 1], st_)

            _op(k, "gpsimd", lambda e: e.iota(t_pd[:, :], pattern=[[1, S]], base=0,
                channel_multiplier=0, allow_small_or_imprecise_dtypes=True), sem=s_pool)
            _op(k, "gpsimd", lambda e: e.iota(t_dp[:, :], pattern=[[0, 1]], base=0,
                channel_multiplier=1, allow_small_or_imprecise_dtypes=True), sem=s_pool)
            _dma(k, "gpsimd", sb_gkv[:, :], d_gkva[:, :], sem=s_ldm)
            if "A" in PHASES:
                _dma(k, "gpsimd", sb_gqa[:, :], d_gqa[:, :], sem=s_ldm)
            ldm_all = s_ldm.n
            pool_all = s_pool.n

            k.wait("vector", s_ldm, ldm_all)
            k.wait("vector", s_pool, pool_all)
            v_ = _op(k, "vector", lambda e: e.tensor_single_scalar(t_dm[:, :], t_dp[:, :], 31.5, ALU.is_ge), sem=s_vc)
            k.wait("vector", s_vc, v_)
            v_ = _op(k, "vector", lambda e: e.tensor_scalar_mul(t_dm[:, :], t_dm[:, :], -32.0), sem=s_vc)
            k.wait("vector", s_vc, v_)
            vc_dm = _op(k, "vector", lambda e: e.tensor_add(t_inv[:, :], t_dm[:, :], t_dp[:, :]), sem=s_vc)
            k.wait("scalar", s_vc, vc_dm)
            sc_inv = _op(k, "scalar", lambda e: e.activation(t_inv[:, :], t_inv[:, :], AF.Exp,
                         scale=-0.28782313662425583), sem=s_sc)
            k.wait("tensor", s_ldm, ldm_all)
            k.wait("tensor", s_pool, pool_all)
            k.wait("tensor", s_vc, vc_ones)
            mm_qb = _mm(k, ps_b1[:, :], sb_ones[0:1, 0:64], sb_qbase[0:1, 0:1], True, True, sem=s_mm)
            k.wait("scalar", s_mm, mm_qb)
            sc_qb = _op(k, "scalar", lambda e: e.copy(t_qb[:, :], ps_b1[:, :]), sem=s_sc)
            k.wait("vector", s_sc, sc_qb)
            _op(k, "vector", lambda e: e.tensor_scalar_add(t_pc[:, :], t_pd[:, :], t_qb[:, 0:1]))
            _op(k, "vector", lambda e: e.tensor_single_scalar(t_pb[:, :], t_pc[:, :], 4095.5, ALU.is_ge))
            _op(k, "vector", lambda e: e.tensor_scalar_mul(t_pb[:, :], t_pb[:, :], -4096.0))
            _op(k, "vector", lambda e: e.tensor_add(t_pa[:, :], t_pb[:, :], t_pc[:, :]))
            k.wait("vector", s_sc, sc_inv)
            vc_pa = _op(k, "vector", lambda e: e.tensor_scalar_mul(t_pa[:, :], t_pa[:, :], t_inv[:, 0:1]), sem=s_vc)
            k.wait("vector", s_vc, vc_pa)

            def wrap_sin(dst, shift):
                if shift:
                    _op(k, "vector", lambda e, sh=shift: e.tensor_scalar_add(t_pc[:, :], t_pa[:, :], sh))
                else:
                    _op(k, "vector", lambda e: e.tensor_copy(t_pc[:, :], t_pa[:, :]))
                _op(k, "vector", lambda e: e.tensor_scalar_mul(t_pb[:, :], t_pc[:, :], INV2PI))
                _op(k, "vector", lambda e: e.tensor_copy(t_ni[:, :], t_pb[:, :]))
                _op(k, "vector", lambda e: e.tensor_copy(t_pb[:, :], t_ni[:, :]))
                _op(k, "vector", lambda e: e.tensor_scalar_mul(t_pb[:, :], t_pb[:, :], -LN2PI))
                v = _op(k, "vector", lambda e: e.tensor_add(t_pd[:, :], t_pb[:, :], t_pc[:, :]), sem=s_vc)
                k.wait("scalar", s_vc, v)
                return _op(k, "scalar", lambda e, d=dst: e.activation(d, t_pd[:, :], AF.Sin), sem=s_sc)

            k.wait("sync", s_vc, s_vc.n)
            tap("freqs", t_pa[:, :], [64, S], F32)
            tap("inv", t_inv[:, :], [64, 1], F32)
            tap("qb", t_qb[:, :], [64, 1], F32)
            wrap_sin(sb_sin[:, :], 0.0)
            sc_tabs = wrap_sin(sb_cos[:, :], np.pi / 2)
            k.wait("sync", s_sc, sc_tabs)
            tap("cosT", sb_cos[:, :], [64, S], BF16)
            tap("sinT", sb_sin[:, :], [64, S], BF16)
            k.wait("gpsimd", s_sc, sc_tabs)
            k.wait("tensor", s_sc, sc_tabs)
            k.wait("vector", s_sc, sc_tabs)
            _dma(k, "gpsimd", d_scr_tb[:, 0:S], sb_sin[:, :], sem=s_gst)
            _dma(k, "gpsimd", d_scr_tb[:, S:2 * S], sb_cos[:, :], sem=s_gst)
            st_.close()

        if "A" in PHASES:
            pa1 = ExitStack(); pa.enter_context(pa1)
            xb = [sb("xb%d" % i, [128, H], F32, pa1) for i in range(3)]
            sb_xnbf = [sb("xnbf%d" % i, [128, H], BF16, pa1) for i in range(2)]
            sb_wkva = sb("wkva_sb", [128, 32 * 576], BF16, pa1)
            t_xt = sb("t_xt", [128, 4096], BF16, pa1)
            t_sum = sb("t_sum", [128, 1], F32, pa1)
            t_ssq = sb("t_ssq", [128, 1], F32, pa1)
            t_mean = sb("t_mean", [128, 1], F32, pa1)
            t_var = sb("t_var", [128, 1], F32, pa1)
            t_m2 = sb("t_m2", [128, 1], F32, pa1)
            t_v1 = sb("t_v1", [128, 1], F32, pa1)
            t_rstd = sb("t_rstd", [128, 1], F32, pa1)
            t_kss = sb("t_kss", [128, 1], F32, pa1)
            t_krc = sb("t_krc", [128, 1], F32, pa1)
            t_kcb = sb("t_kcb", [128, 512], BF16, pa1)
            kcst = [sb("kcst%d" % i, [128, 512], BF16, pa1) for i in range(2)]
            t_krb = sb("t_krb", [128, 64], BF16, pa1)
            t_krr = sb("t_krr", [64, 128], BF16, pa1)
            t_krrot = sb("t_krrot", [64, 128], BF16, pa1)
            t_kr2 = sb("t_kr2", [64, 128], BF16, pa1)
            t_kr3 = sb("t_kr3", [64, 128], BF16, pa1)
            t_junk = sb("t_junk", [128, 1], F32, pa1)
            ps_tr = [psum("ps_tr%d" % i, [128, 512], pa1) for i in range(2)]
            ps_kv = psum("ps_kv", [128, 512], pa1)
            ps_kv2 = psum("ps_kv2", [128, 64], pa1)
            ps_ktr = psum("ps_ktr", [128, 512], pa1)
            ps_krtr = psum("ps_krtr", [64, 128], pa1)

            for cc in range(4):
                _dma(k, "gpsimd", sb_wkva[:, cc * 8 * 576:(cc + 1) * 8 * 576],
                     d_wkva[:, cc * 8:(cc + 1) * 8, :].rearrange("p a b -> p (a b)"), sem=s_ldw)
            ldw_wkva = s_ldw.n

            xfree_v = {}; xfree_s = {}
            kst = {}
            kc_ev = {}
            trgrp = 0
            sc2_evgrp = 0
            trl = {}
            prev = {"vc_kcb": 0, "sc_kcT": 0, "sc_krb": 0,
                    "mm_ktr": 0, "mm_krtr": 0, "vc_krT": 0, "sc_krr": 0, "mm_kv": 0}

            for i in range(NT):
                bi = i % 3
                my = i < NQT
                xnb = sb_xnbf[i % 2]
                if i >= 3:
                    k.wait(XQ, s_vc, xfree_v[i - 3])
                    k.wait(XQ, s_sc, xfree_s[i - 3])
                if SERIAL_A1:
                    k.wait(XQ, s_mm, s_mm.n)
                for q4 in range(8):
                    _dma(k, XQ, xb[bi][:, q4 * 512:(q4 + 1) * 512],
                         d_x[i * 128:(i + 1) * 128, q4 * 512:(q4 + 1) * 512], sem=s_ldx)
                ldx_i = s_ldx.n
                if i >= 1:
                    k.wait(XQ, s_sc3, kc_ev[i - 1])
                    _dma(k, XQ,
                         d_scr_kc[:, :].rearrange("p (a s) -> p a s", s=S)
                         [:, :, (i - 1) * 128:i * 128],
                         kcst[(i - 1) % 2][:, :].rearrange("p (a f) -> p a f", f=128),
                         sem=s_gst)
                    kst[i - 1] = s_gst.n
                k.wait("vector", s_ldx, ldx_i)
                vc_red = _op(k, "vector", lambda e, b=bi: e.reduce_sum(t_sum[:, :], xb[b][:, :], axis=AX.X), sem=s_vc)
                k.wait("vector", s_vc, vc_red)
                k.wait("scalar", s_ldx, ldx_i)
                if i >= 2 and trl[i - 2]:
                    k.wait("scalar", s_mm, trl[i - 2])
                sc_sq = _op(k, "scalar", lambda e, b=bi, xnb=xnb: e.activation(
                    xnb[:, :], xb[b][:, :], AF.Square, accum_out=t_ssq[:, :]), sem=s_sc)
                v_ = _op(k, "vector", lambda e: e.tensor_scalar_mul(t_mean[:, :], t_sum[:, :], 1.0 / H), sem=s_vc)
                k.wait("vector", s_vc, v_)
                v_ = _op(k, "vector", lambda e: e.tensor_mul(t_m2[:, :], t_mean[:, :], t_mean[:, :]), sem=s_vc)
                k.wait("vector", s_vc, v_)
                k.wait("vector", s_sc, sc_sq)
                v_ = _op(k, "vector", lambda e: e.tensor_scalar_mul(t_v1[:, :], t_ssq[:, :], 1.0 / H), sem=s_vc)
                k.wait("vector", s_vc, v_)
                v_ = _op(k, "vector", lambda e: e.tensor_sub(t_var[:, :], t_v1[:, :], t_m2[:, :]), sem=s_vc)
                k.wait("vector", s_vc, v_)
                vc_var = _op(k, "vector", lambda e: e.tensor_scalar_add(t_var[:, :], t_var[:, :], 1e-5), sem=s_vc)
                k.wait("scalar", s_vc, vc_var)
                s_ = _op(k, "scalar", lambda e: e.activation(t_rstd[:, :], t_var[:, :], AF.Ln), sem=s_sc)
                k.wait("scalar", s_sc, s_)
                sc_rstd = _op(k, "scalar", lambda e: e.activation(
                    t_rstd[:, :], t_rstd[:, :], AF.Exp, scale=-0.5), sem=s_sc)
                k.wait("vector", s_sc, sc_rstd)
                if i >= 2 and trl[i - 2]:
                    k.wait("vector", s_mm, trl[i - 2])
                vc_xn = _op(k, "vector", lambda e, b=bi, xnb=xnb: e.tensor_scalar(
                    xnb[:, :], xb[b][:, :], t_mean[:, 0:1], t_rstd[:, 0:1],
                    ALU.subtract, ALU.mult), sem=s_vc)
                if i == 0 and "xn0" in TAPS:
                    k.wait("sync", s_vc, vc_xn)
                    tap("xn0", xnb[:, :], [128, H], BF16)
                    tap("sum0", t_sum[:, :], [128, 1], F32)
                    tap("ssq0", t_ssq[:, :], [128, 1], F32)
                    tap("mean0", t_mean[:, :], [128, 1], F32)
                    tap("var0", t_var[:, :], [128, 1], F32)
                    tap("rstd0", t_rstd[:, :], [128, 1], F32)
                    k.wait("vector", s_st, s_st.n)
                    k.wait("scalar", s_st, s_st.n)
                    k.wait("tensor", s_st, s_st.n)
                xfree_v[i] = vc_xn
                xfree_s[i] = sc_sq
                if SERIAL_A1:
                    k.wait("tensor", s_vc, s_vc.n)
                    k.wait("tensor", s_sc, s_sc.n)
                    k.wait("tensor", s_sc2, s_sc2.n)
                    k.wait("tensor", s_sc3, s_sc3.n)
                    k.wait("tensor", s_sc4, s_sc4.n)
                k.wait("tensor", s_vc, vc_xn)
                for g in range(8):
                    trgrp += 1
                    if trgrp > 2:
                        k.wait("tensor", s_sc2, trgrp - 2)
                    mmg = None
                    for c4 in range(4):
                        c = g * 4 + c4
                        mmg = _mm(k, ps_tr[g % 2][:, c4 * 128:(c4 + 1) * 128],
                                  xnb[:, c * 128:(c + 1) * 128], sb_ident[:, :],
                                  True, True, sem=(s_mm if c4 == 3 else None))
                    k.wait("scalar", s_mm, mmg)
                    if my:
                        qg, qc = (0 if (i % 8) < 4 else 1), (i % 4)
                        _op(k, "scalar", lambda e, qg=qg, qc=qc, g=g: e.copy(
                            sb_xnT[qg][:, :].rearrange("p (c b) -> p c b", b=512)
                                [:, 4 * g:4 * g + 4, qc * 128:(qc + 1) * 128],
                            ps_tr[g % 2][:, :].rearrange("p (a f) -> p a f", f=128)),
                            sem=s_sc2)
                    else:
                        _op(k, "scalar", lambda e, g=g: e.copy(
                            t_xt[:, g * 512:(g + 1) * 512], ps_tr[g % 2][:, :]), sem=s_sc2)
                    sc2_evgrp += 1
                trl[i] = s_mm.n
                if my:
                    qg, qc = (0 if (i % 8) < 4 else 1), (i % 4)
                    last_dst = sb_xnT[qg][:, 31 * 512 + qc * 128 + 127: 31 * 512 + qc * 128 + 128]
                else:
                    last_dst = t_xt[:, 4095:4096]
                sc_settle = _op(k, "scalar", lambda e, d=last_dst: e.copy(t_junk[:, :], d), sem=s_sc)
                k.wait("tensor", s_ldw, ldw_wkva)
                k.wait("tensor", s_sc2, sc2_evgrp)
                k.wait("tensor", s_sc, sc_settle)
                if prev["vc_kcb"]:
                    k.wait("tensor", s_vc, prev["vc_kcb"])
                if prev["sc_krb"]:
                    k.wait("tensor", s_sc, prev["sc_krb"])
                mm_kv = mm_kv2 = None
                for c in range(32):
                    if my:
                        qg, qc = (0 if (i % 8) < 4 else 1), (i % 4)
                        stat = sb_xnT[qg][:, c * 512 + qc * 128: c * 512 + (qc + 1) * 128]
                    else:
                        stat = t_xt[:, c * 128:(c + 1) * 128]
                    mm_kv = _mm(k, ps_kv[:, :], stat, sb_wkva[:, c * 576: c * 576 + 512],
                                c == 0, c == 31, sem=(s_mm if c == 31 else None))
                    mm_kv2 = _mm(k, ps_kv2[:, :], stat, sb_wkva[:, c * 576 + 512:(c + 1) * 576],
                                 c == 0, c == 31, sem=(s_mm2 if c == 31 else None))
                prev["mm_kv"] = mm_kv
                k.wait("scalar", s_mm, mm_kv)
                sc_acc = _op(k, "scalar", lambda e: e.activation(t_xt[:, 0:512], ps_kv[:, :], AF.Square,
                                                        accum_out=t_kss[:, :]), sem=s_sc)
                k.wait("scalar", s_sc, sc_acc)
                s_ = _op(k, "scalar", lambda e: e.activation(t_kss[:, :], t_kss[:, :], AF.Ln), sem=s_sc)
                k.wait("scalar", s_sc, s_)
                sc_nrm = _op(k, "scalar", lambda e: e.activation(
                    t_kss[:, :], t_kss[:, :], AF.Exp, scale=0.5), sem=s_sc)
                k.wait("vector", s_sc, sc_nrm)
                v_ = _op(k, "vector", lambda e: e.tensor_scalar_add(t_kss[:, :], t_kss[:, :], 2.2627416997969522e-05), sem=s_vc)
                k.wait("vector", s_vc, v_)
                vc_rcp = _op(k, "vector", lambda e: e.reciprocal(t_krc[:, :], t_kss[:, :]), sem=s_vc)
                k.wait("vector", s_vc, vc_rcp)
                if prev["mm_ktr"]:
                    k.wait("vector", s_mm, prev["mm_ktr"])
                _op(k, "vector", lambda e: e.tensor_scalar(t_kcb[:, :], ps_kv[:, :],
                    t_krc[:, 0:1], 22.627416997969522, ALU.mult, ALU.mult))
                vc_kcb = _op(k, "vector", lambda e: e.tensor_copy(t_junk[:, :], t_kcb[:, 511:512]), sem=s_vc)
                prev["vc_kcb"] = vc_kcb
                k.wait("tensor", s_vc, vc_kcb)
                if prev["sc_kcT"]:
                    k.wait("tensor", s_sc3, prev["sc_kcT"])
                mm_ktr = None
                for cc in range(4):
                    mm_ktr = _mm(k, ps_ktr[:, cc * 128:(cc + 1) * 128],
                                 t_kcb[:, cc * 128:(cc + 1) * 128], sb_ident[:, :],
                                 True, True, sem=(s_mm if cc == 3 else None))
                prev["mm_ktr"] = mm_ktr
                k.wait("scalar", s_mm, mm_ktr)
                if i >= 2:
                    k.wait("scalar", s_gst, kst[i - 2])
                sc_kcT = None
                for cc in range(4):
                    sc_kcT = _op(k, "scalar", lambda e, cc=cc, i=i: e.activation(
                        kcst[i % 2][:, cc * 128:(cc + 1) * 128],
                        ps_ktr[:, cc * 128:(cc + 1) * 128], AF.Copy,
                        scale=sb_gkv[:, cc:cc + 1]), sem=(s_sc3 if cc == 3 else None))
                prev["sc_kcT"] = sc_kcT
                kc_ev[i] = sc_kcT
                k.wait("scalar", s_mm2, mm_kv2)
                if prev["mm_krtr"]:
                    k.wait("scalar", s_mm, prev["mm_krtr"])
                _op(k, "scalar", lambda e: e.copy(t_krb[:, :], ps_kv2[:, :]))
                sc_krb = _op(k, "scalar", lambda e: e.copy(t_junk[:, :], t_krb[:, 63:64]), sem=s_sc)
                prev["sc_krb"] = sc_krb
                k.wait("tensor", s_sc, sc_krb)
                if prev["sc_krr"]:
                    k.wait("tensor", s_sc4, prev["sc_krr"])
                mm_krtr = _mm(k, ps_krtr[:, :], t_krb[:, :], sb_ident[:, :], True, True, sem=s_mm)
                prev["mm_krtr"] = mm_krtr
                k.wait("scalar", s_mm, mm_krtr)
                if prev["vc_krT"]:
                    k.wait("scalar", s_vc, prev["vc_krT"])
                sc_krr = _op(k, "scalar", lambda e: e.copy(t_krr[:, :], ps_krtr[:, :]), sem=s_sc4)
                prev["sc_krr"] = sc_krr
                k.wait("vector", s_sc4, sc_krr)
                _op(k, "vector", lambda e: e.tensor_scalar_mul(t_krrot[0:32, :], t_krr[32:64, :], -1.0))
                _op(k, "vector", lambda e: e.tensor_copy(t_krrot[32:64, :], t_krr[0:32, :]))
                _op(k, "vector", lambda e, i=i: e.tensor_mul(t_kr2[:, :], t_krr[:, :], sb_cos[:, i * 128:(i + 1) * 128]))
                _op(k, "vector", lambda e, i=i: e.tensor_mul(t_kr3[:, :], t_krrot[:, :], sb_sin[:, i * 128:(i + 1) * 128]))
                vc_krT = _op(k, "vector", lambda e, i=i: e.tensor_add(
                    sb_krT[:, i * 128:(i + 1) * 128], t_kr2[:, :], t_kr3[:, :]), sem=s_vc)
                prev["vc_krT"] = vc_krT

            k.wait("sync", s_vc, s_vc.n)
            k.wait("sync", s_sc2, s_sc2.n)
            k.wait("sync", s_sc3, s_sc3.n)
            k.wait("sync", s_sc4, s_sc4.n)
            k.wait(XQ, s_sc3, kc_ev[NT - 1])
            _dma(k, XQ,
                 d_scr_kc[:, :].rearrange("p (a s) -> p a s", s=S)
                 [:, :, (NT - 1) * 128:NT * 128],
                 kcst[(NT - 1) % 2][:, :].rearrange("p (a f) -> p a f", f=128),
                 sem=s_gst)
            kst[NT - 1] = s_gst.n
            k.wait(XQ, s_gst, kst[NT - 1])
            k.wait("sync", s_sc, s_sc.n)
            tap("xn31", sb_xnbf[1][:, :], [128, H], BF16)
            tap("txt31", t_xt[:, :], [128, 4096], BF16)
            tap("xnT0", sb_xnT[0][:, :], [128, 32 * 512], BF16)
            tap("xnT1", sb_xnT[1][:, :], [128, 32 * 512], BF16)
            tap("krT", sb_krT[:, :], [64, S], BF16)
            pa1.close()

        if "A" in PHASES:
            pa2 = ExitStack(); pa.enter_context(pa2)
            sb_qc = sb("qc_bf", [128, 8 * 1536], BF16, pa2)
            sb_qcT = sb("qcT", [128, 12 * 1024], BF16, pa2)
            wqab = [sb("wqab%d" % i, [128, 4 * 512], BF16, pa2) for i in range(4)]
            t_qss = sb("t_qss", [128, 8], F32, pa2)
            t_qs3 = sb("t_qs3", [128, 24], F32, pa2)
            t_qp = sb("t_qp", [128, 1], F32, pa2)
            t_qd = sb("t_qd", [128, 512], BF16, pa2)
            t_qn = sb("t_qn", [128, 1], F32, pa2)
            t_qr = sb("t_qr", [128, 8], F32, pa2)
            pq = ExitStack(); pa2.enter_context(pq)
            ps_q = [psum("ps_q%d" % i, [128, 512], pq) for i in range(8)]

            ldw_base = s_ldw.n
            def wqa_load(n):
                t, cg = n // 8, n % 8
                for hh2 in range(2):
                    _dma(k, "gpsimd", wqab[n % 4][:, hh2 * 1024:(hh2 + 1) * 1024],
                         d_wqa[t, :, cg * 2048 + hh2 * 1024: cg * 2048 + (hh2 + 1) * 1024],
                         sem=s_ldw)
            k.wait("gpsimd", s_sc4, s_sc4.n)
            k.wait("gpsimd", s_vc, s_vc.n)
            k.wait("tensor", s_sc, s_sc.n)
            k.wait("tensor", s_sc3, s_sc3.n)
            k.wait("tensor", s_sc4, s_sc4.n)
            k.wait("tensor", s_vc, s_vc.n)
            for n0 in range(4):
                wqa_load(n0)
            grp_mm = {}
            prev_evq = 0
            for t3 in range(3):
                mm_q = None
                for cg in range(8):
                    n = t3 * 8 + cg
                    k.wait("tensor", s_ldw, ldw_base + 32 * min(n + 2, 24))
                    for c4 in range(4):
                        c = cg * 4 + c4
                        if t3 > 0 and cg == 0 and c4 == 0:
                            k.wait("tensor", s_sc3, prev_evq)
                        for qt in range(8):
                            qg, qc = qt // 4, qt % 4
                            stat = sb_xnT[qg][:, c * 512 + qc * 128: c * 512 + (qc + 1) * 128]
                            last = (c4 == 3 and qt == 7)
                            mm_q = _mm(k, ps_q[qt][:, :], stat,
                                       wqab[n % 4][:, c4 * 512:(c4 + 1) * 512],
                                       c == 0, c == 31,
                                       sem=(s_mm2 if last else None))
                    grp_mm[n] = s_mm2.n
                    if n + 4 <= 23:
                        k.wait("gpsimd", s_mm2, grp_mm[n])
                        wqa_load(n + 4)
                k.wait("scalar", s_mm2, grp_mm[t3 * 8 + 7])
                sc_ev = None
                for qt in range(8):
                    sc_acc = _op(k, "scalar", lambda e, qt=qt: e.activation(
                        t_qd[:, :], ps_q[qt][:, :], AF.Square, accum_out=t_qp[:, :]), sem=s_sc)
                    k.wait("scalar", s_sc, sc_acc)
                    k.wait("vector", s_sc, sc_acc)
                    vq = _op(k, "vector", lambda e, qt=qt, t3=t3: e.tensor_copy(
                        t_qs3[:, qt * 3 + t3: qt * 3 + t3 + 1], t_qp[:, :]), sem=s_vc)
                    k.wait("vector", s_vc, vq)
                    k.wait("scalar", s_vc, vq)
                    sc_ev = _op(k, "scalar", lambda e, qt=qt, t3=t3: e.copy(
                        sb_qc[:, qt * 1536 + t3 * 512: qt * 1536 + (t3 + 1) * 512],
                        ps_q[qt][:, :]), sem=s_sc3)
                prev_evq = sc_ev
            pq.close()
            k.wait("sync", s_sc3, s_sc3.n)
            k.wait("sync", s_vc, s_vc.n)
            tap("qcraw", sb_qc[:, :], [128, 8 * 1536], BF16)

            pq2 = ExitStack(); pa2.enter_context(pq2)
            ps_qtr = [psum("ps_qtr%d" % i, [128, 512], pq2) for i in range(2)]
            t_diag = sb("t_diag", [128, 8 * 128], BF16, pq2)
            k.wait("tensor", s_sc3, prev_evq)
            k.wait("vector", s_sc3, prev_evq)
            k.wait("scalar", s_vc, s_vc.n)
            for qt in range(8):
                v_ = _op(k, "vector", lambda e, qt=qt: e.tensor_add(
                    t_qss[:, qt:qt + 1], t_qs3[:, qt * 3: qt * 3 + 1], t_qs3[:, qt * 3 + 1: qt * 3 + 2]), sem=s_vc)
                k.wait("vector", s_vc, v_)
                v_ = _op(k, "vector", lambda e, qt=qt: e.tensor_add(
                    t_qn[:, :], t_qss[:, qt:qt + 1], t_qs3[:, qt * 3 + 2: qt * 3 + 3]), sem=s_vc)
                k.wait("scalar", s_vc, v_)
                s_ = _op(k, "scalar", lambda e: e.activation(t_qn[:, :], t_qn[:, :], AF.Ln), sem=s_sc)
                k.wait("scalar", s_sc, s_)
                s_ = _op(k, "scalar", lambda e: e.activation(t_qn[:, :], t_qn[:, :], AF.Exp, scale=0.5), sem=s_sc)
                k.wait("vector", s_sc, s_)
                v_ = _op(k, "vector", lambda e: e.tensor_scalar_add(t_qn[:, :], t_qn[:, :], 3.9191835884530846e-05), sem=s_vc)
                k.wait("vector", s_vc, v_)
                v_ = _op(k, "vector", lambda e, qt=qt: e.reciprocal(t_qr[:, qt:qt + 1], t_qn[:, :]), sem=s_vc)
                k.wait("vector", s_vc, v_)
                _op(k, "vector", lambda e, qt=qt: e.tensor_scalar(
                    t_diag[:, qt * 128:(qt + 1) * 128], sb_ident[:, :],
                    t_qr[:, qt:qt + 1], 39.191835884530846, ALU.mult, ALU.mult))
            vset = _op(k, "vector", lambda e: e.tensor_copy(t_qp[:, :], t_diag[:, 1023:1024]), sem=s_vc)
            k.wait("sync", s_vc, vset)
            tap("diag", t_diag[:, :], [128, 1024], BF16)
            k.wait("tensor", s_vc, vset)
            ntr = 0
            prev_ev2 = {}
            dmm = {}
            for qt in range(8):
                for g in range(3):
                    ntr += 1
                    if ntr > 2:
                        k.wait("tensor", s_sc3, prev_ev2[ntr - 2])
                    mmg = None
                    for c4 in range(4):
                        c = g * 4 + c4
                        mmg = _mm(k, ps_qtr[ntr % 2][:, c4 * 128:(c4 + 1) * 128],
                                  sb_qc[:, qt * 1536 + c * 128: qt * 1536 + (c + 1) * 128],
                                  t_diag[:, qt * 128:(qt + 1) * 128], True, True, sem=(s_mm2 if c4 == 3 else None))
                    k.wait("scalar", s_mm2, mmg)
                    ev = None
                    for c4 in range(4):
                        c = g * 4 + c4
                        ev = _op(k, "scalar", lambda e, c=c, qt=qt, c4=c4, ntr=ntr: e.activation(
                            sb_qcT[:, c * 1024 + qt * 128: c * 1024 + (qt + 1) * 128],
                            ps_qtr[ntr % 2][:, c4 * 128:(c4 + 1) * 128], AF.Copy,
                            scale=sb_gqa[:, c:c + 1]), sem=(s_sc3 if c4 == 3 else None))
                    prev_ev2[ntr] = ev
                dmm[qt] = s_mm2.n
            pq2.close()
            k.wait("sync", s_sc3, s_sc3.n)
            tap("gqa", sb_gqa[:, :], [128, 12], F32)
            tap("qcT", sb_qcT[:, :], [128, 12 * 1024], BF16)

            if "C" in PHASES or "SPILL" in PHASES:
                k.wait("gpsimd", s_sc3, s_sc3.n)
                k.wait("gpsimd", s_vc, s_vc.n)
                _dma(k, "gpsimd", d_scr_kr[:, :], sb_krT[:, :], sem=s_gst)
                _dma(k, "gpsimd", d_scr_qc[:, :], sb_qcT[:, :], sem=s_gst)
                k.wait("gpsimd", s_gst, s_gst.n)

        if "B" in PHASES:
            pa.close()
            k.wait("gpsimd", s_sc3, s_sc3.n)
            k.wait("gpsimd", s_vc, s_vc.n)
            k.wait("gpsimd", s_mm2, s_mm2.n)
            pb = ExitStack(); es.enter_context(pb)
            sb_gsv = sb("gs_sb", [128, NF], F32, pb)
            sb_usv = sb("us_sb", [128, NF], F32, pb)
            sb_sd = sb("sd_bc", [128, H], BF16, pb)
            sb_hT = sb("hT", [128, NF * 512], BF16, pb)
            t_sg = sb("t_sg", [128, 512], F32, pb)
            t_su = sb("t_su", [128, 512], F32, pb)
            g_bf = [sb("g_bf%d" % i, [128, 512], BF16, pb) for i in range(2)]
            t_jk = sb("t_jkb", [128, 4], BF16, pb)

            _dma(k, "gpsimd", sb_gsv[:, :], d_gs[:, :], sem=s_ldm)
            _dma(k, "gpsimd", sb_usv[:, :], d_us[:, :], sem=s_ldm)
            ldm_b = s_ldm.n

            pbc = ExitStack(); pb.enter_context(pbc)
            ps_bc = psum("ps_bc2", [128, 512], pbc)
            for j in range(8):
                ldj = _dma(k, "gpsimd", t_su[0:1, :], d_dns[:, j * 512:(j + 1) * 512], sem=s_ldm)
                k.wait("tensor", s_ldm, ldj)
                if j > 0:
                    k.wait("tensor", s_sc, sc_bc)
                mmj = _mm(k, ps_bc[:, :], sb_ones[0:1, :], t_su[0:1, :], True, True, sem=s_mm)
                k.wait("scalar", s_mm, mmj)
                sc_bc = _op(k, "scalar", lambda e, j=j: e.copy(sb_sd[:, j * 512:(j + 1) * 512], ps_bc[:, :]), sem=s_sc)
                k.wait("gpsimd", s_sc, sc_bc)
            pbc.close()

            pbg = ExitStack(); pb.enter_context(pbg)
            gbuf = [sb("gbuf%d" % i, [128, 4096], BF16, pbg) for i in range(2)]
            ubuf = [sb("ubuf%d" % i, [128, 4096], BF16, pbg) for i in range(2)]
            pb1 = ExitStack(); pb.enter_context(pb1)
            ps_g = [psum("ps_g%d" % i, [128, 512], pb1) for i in range(4)]
            ps_u = [psum("ps_u%d" % i, [128, 512], pb1) for i in range(4)]
            h1b = [sb("h1b%d" % i, [128, 512], BF16, pb1) for i in range(2)]
            ld0 = s_ldw.n
            fmm = {}
            fsg = {}
            fsu = {}
            hst = {0: 0}
            def b1_load(f):
                _dma(k, "gpsimd", gbuf[f % 2][:, :], d_gate[f], sem=s_ldw)
                _dma(k, "gpsimd", ubuf[f % 2][:, :], d_up[f], sem=s_ldw)
            b1_load(0)
            b1_load(1)
            for f in range(NF):
                k.wait("tensor", s_ldw, ld0 + 32 * (f + 1 if f > 0 else 2))
                if f >= 2:
                    k.wait("tensor", s_vc, fsu[f - 2])
                mmf = None
                for qg in range(2):
                    for c in range(32):
                        _mm(k, ps_g[qg * 2 + f % 2][:, :], gbuf[f % 2][:, c * 128:(c + 1) * 128],
                            sb_xnT[qg][:, c * 512:(c + 1) * 512], c == 0, c == 31)
                    for c in range(32):
                        mmf = _mm(k, ps_u[qg * 2 + f % 2][:, :], ubuf[f % 2][:, c * 128:(c + 1) * 128],
                                  sb_xnT[qg][:, c * 512:(c + 1) * 512], c == 0, c == 31,
                                  sem=(s_mm if (qg == 1 and c == 31) else None))
                fmm[f] = mmf
                if f >= 1:
                    k.wait("gpsimd", s_vc, fsu[f - 1])
                    _dma(k, "gpsimd", d_scr_h[:, (f - 1) * 512: f * 512], h1b[(f - 1) % 2][:, :], sem=s_gst)
                    hst[f - 1] = s_gst.n
                if f + 2 < NF:
                    k.wait("gpsimd", s_mm, fmm[f])
                    b1_load(f + 2)
                k.wait("scalar", s_mm, fmm[f])
                if f >= 1:
                    k.wait("scalar", s_vc, fsg[f - 1])
                sg1 = _op(k, "scalar", lambda e, f=f: e.activation(
                    t_sg[:, :], ps_g[f % 2][:, :], AF.Silu, scale=sb_gsv[:, f:f + 1]), sem=s_sc)
                k.wait("vector", s_mm, fmm[f])
                k.wait("vector", s_sc, sg1)
                _op(k, "vector", lambda e, f=f: e.tensor_scalar_mul(
                    t_su[:, :], ps_u[f % 2][:, :], sb_usv[:, f:f + 1]))
                vh0 = _op(k, "vector", lambda e, f=f: e.tensor_mul(
                    sb_hT[:, f * 512:(f + 1) * 512], t_su[:, :], t_sg[:, :]), sem=s_vc)
                k.wait("scalar", s_vc, vh0)
                sg2 = _op(k, "scalar", lambda e, f=f: e.activation(
                    t_sg[:, :], ps_g[2 + f % 2][:, :], AF.Silu, scale=sb_gsv[:, f:f + 1]), sem=s_sc)
                k.wait("vector", s_sc, sg2)
                if f >= 2:
                    k.wait("vector", s_gst, hst[f - 2] if (f - 2) in hst else 0)
                _op(k, "vector", lambda e, f=f: e.tensor_scalar_mul(
                    t_su[:, :], ps_u[2 + f % 2][:, :], sb_usv[:, f:f + 1]))
                fsu[f] = _op(k, "vector", lambda e, f=f: e.tensor_mul(
                    h1b[f % 2][:, :], t_su[:, :], t_sg[:, :]), sem=s_vc)
                fsg[f] = fsu[f]
                if f == NF - 1:
                    k.wait("gpsimd", s_vc, fsu[f])
                    _dma(k, "gpsimd", d_scr_h[:, f * 512:(f + 1) * 512], h1b[f % 2][:, :], sem=s_gst)
                    hst[f] = s_gst.n
            pb1.close()
            pbg.close()
            pbd = ExitStack(); pb.enter_context(pbd)
            FCH = 8
            NDB = 4
            dbuf = [sb("dbuf%d" % i, [128, FCH * 512], BF16, pbd) for i in range(NDB)]

            for qg in range(2):
                if qg == 1:
                    k.wait("gpsimd", s_mm2, s_mm2.n)
                    hrel = _dma(k, "gpsimd", sb_hT[:, :], d_scr_h[:, :], sem=s_ldw)
                    k.wait("tensor", s_ldw, s_ldw.n)
                pb2 = ExitStack(); pb.enter_context(pb2)
                ps_d = [psum("ps_d%d_%d" % (qg, i), [128, 512], pb2) for i in range(8)]
                k.wait("tensor", s_vc, fsu[NF - 1])
                ld0 = s_ldw.n
                nfb = (NF + FCH - 1) // FCH
                njt = 8 * nfb
                def b2_load(j):
                    dg2, fb = j // nfb, j % nfb
                    nf = min(FCH, NF - fb * FCH)
                    _dma(k, "gpsimd", dbuf[j % NDB][:, 0:nf * 512],
                         d_down[dg2, :, fb * FCH * 512:(fb * FCH + nf) * 512], sem=s_ldw)
                dval = {}
                mmj = {}
                if not SERIAL_B2:
                    for j0 in range(NDB):
                        b2_load(j0)
                gst_base = s_gst.n
                evd = {}
                for dg2 in range(8):
                    pset = (dg2 % 2) * 4
                    for fb in range(nfb):
                        j = dg2 * nfb + fb
                        if SERIAL_B2:
                            k.wait("gpsimd", s_mm2, s_mm2.n)
                            b2_load(j)
                            nwait = j + 1
                        else:
                            nwait = j + 2
                        k.wait("tensor", s_ldw,
                               ld0 + min(nwait, njt) * 16 + (16 if nwait > njt else 0))
                        nf = min(FCH, NF - fb * FCH)
                        mmfb = None
                        for f4 in range(nf):
                            f = fb * FCH + f4
                            for qt2 in range(4):
                                if f == 0 and qt2 == 0 and dg2 >= 2:
                                    k.wait("tensor", s_vc, evd[dg2 - 2])
                                mmfb = _mm(k, ps_d[pset + qt2][:, :],
                                    sb_hT[:, f * 512 + qt2 * 128: f * 512 + (qt2 + 1) * 128],
                                    dbuf[j % NDB][:, f4 * 512:(f4 + 1) * 512],
                                    f == 0, f == NF - 1,
                                    sem=(s_mm2 if (f4 == nf - 1 and qt2 == 3) else None))
                        mmj[j] = mmfb
                        if not SERIAL_B2:
                            nxt = j + NDB
                            if nxt <= njt - 1:
                                k.wait("gpsimd", s_mm2, mmj[nxt - NDB])
                                b2_load(nxt)
                            elif nxt == njt:
                                _dma(k, "gpsimd", t_jk[:, :], d_down[0, :, 0:4], sem=s_ldw)
                    dval[dg2] = s_mm2.n
                    k.wait("vector", s_mm2, dval[dg2])
                    ev = None
                    for qt2 in range(4):
                        v_ = _op(k, "vector", lambda e, qt2=qt2, pset=pset, dg2=dg2: e.tensor_mul(
                            t_su[:, :], ps_d[pset + qt2][:, :], sb_sd[:, dg2 * 512:(dg2 + 1) * 512]), sem=s_vc)
                        k.wait("scalar", s_vc, v_)
                        if dg2 * 4 + qt2 >= 2:
                            k.wait("scalar", s_gst, gst_base + 16 * (dg2 * 4 + qt2 - 1))
                        ev = _op(k, "scalar", lambda e, qt2=qt2: e.activation(
                            g_bf[qt2 % 2][:, :], t_su[:, :], AF.Silu), sem=s_sc)
                        k.wait("gpsimd", s_sc, ev)
                        _dma(k, "gpsimd", d_scr_g[qg * 512 + qt2 * 128: qg * 512 + (qt2 + 1) * 128,
                                                  dg2 * 512:(dg2 + 1) * 512], g_bf[qt2 % 2][:, :], sem=s_gst)
                        k.wait("vector", s_sc, ev)
                    evd[dg2] = s_vc.n
                pb2.close()
            k.wait("gpsimd", s_gst, s_gst.n)
            pb.close()

        if "C" in PHASES:
            if "B" in PHASES:
                pb.close()
            p1.close()
            k.wait("gpsimd", s_mm2, s_mm2.n)
            k.wait("gpsimd", s_sc, s_sc.n)
            k.wait("gpsimd", s_vc, s_vc.n)
            pc = ExitStack(); es.enter_context(pc)
            sb_aoT = sb("aoT", [128, 32 * 1024], BF16, pc)
            csub = ExitStack(); pc.enter_context(csub)
            sb_cos2 = sb("cosT2", [64, S], BF16, csub)
            sb_sin2 = sb("sinT2", [64, S], BF16, csub)
            _dma(k, "gpsimd", sb_sin2[:, :], d_scr_tb[:, 0:S], sem=s_ldw)
            _dma(k, "gpsimd", sb_cos2[:, :], d_scr_tb[:, S:2 * S], sem=s_ldw)
            ldtb = s_ldw.n

            sb_kcT2 = sb("kcT2", [128, 4 * S], BF16, csub)
            sb_krT2 = sb("krT2", [128, S], BF16, csub)
            sb_qcT2 = sb("qcT2", [128, 12 * 1024], BF16, csub)
            wqbb = [sb("wqbb%d" % i, [128, 12 * 192 + 64], BF16, csub) for i in range(2)]
            wkvbb = [sb("wkvbb%d" % i, [128, 4 * 256], BF16, csub) for i in range(2)]
            sb_knT = sb("knT", [128, S], BF16, csub)
            sb_vaug = sb("vaug", [128, 32 * 128], BF16, csub)
            sb_exp = sb("expT", [128, 4 * 512], BF16, csub)
            sb_eA = sb("eA", [128, 512], BF16, csub)
            sb_eB = sb("eB", [128, 512], BF16, csub)
            sb_eG = [sb("eG%d" % i, [128, 512], BF16, csub) for i in range(2)]
            sb_qnT = sb("qnT", [128, 1024], BF16, csub)
            sb_qrT = sb("qrT", [128, 1024], BF16, csub)
            t_qro = sb("t_qro", [64, 1024], BF16, csub)
            t_qrb = sb("t_qrb", [64, 1024], BF16, csub)
            t_qr2 = sb("t_qr2c", [64, 1024], BF16, csub)
            t_qr3 = sb("t_qr3c", [64, 1024], BF16, csub)
            sb_one2 = sb("one2", [128, 128], BF16, csub)
            t_bc = sb("t_bc", [128, 512], F32, csub)

            _dma(k, "gpsimd", sb_kcT2[:, :], d_scr_kc[:, :], sem=s_ldw)
            _dma(k, "gpsimd", sb_krT2[0:64, :], d_scr_kr[:, :], sem=s_ldw)
            _dma(k, "gpsimd", sb_qcT2[:, :], d_scr_qc[:, :], sem=s_ldw)
            _op(k, "vector", lambda e: e.memset(sb_one2[:, :], 1.0))
            _op(k, "vector", lambda e: e.memset(sb_krT2[64:128, :], 0.0))
            _op(k, "vector", lambda e: e.memset(sb_qrT[64:128, :], 0.0))
            v_ones = _op(k, "vector", lambda e: e.memset(
                wqbb[0][:, 2304:2368], 0.0), sem=s_vc)
            _op(k, "vector", lambda e: e.memset(wqbb[1][:, 2304:2368], 0.0), sem=s_vc)

            def c_load(h):
                _dma(k, "gpsimd", wqbb[h % 2][:, 0:2304].rearrange("p (a b) -> p a b", a=12),
                     d_wqb[:, h], sem=s_ldw)
                _dma(k, "gpsimd", wkvbb[h % 2][:, :].rearrange("p (a b) -> p a b", a=4),
                     d_wkvb[:, h], sem=s_ldw)
            ldc0 = s_ldw.n
            c_load(0)
            c_load(1)

            ppc = ExitStack(); csub.enter_context(ppc)
            pb_ = [psum("pbank%d" % i, [128, 512], ppc) for i in range(8)]

            hp_qn = {}; hp_qr = {}; hp_wfree = {}
            fin_sc = {0: 0}; fin_vc = {0: 0}
            last_mmb = [0]
            exp_hist = []
            att_hist = []

            for h in range(NH):
                k.wait("tensor", s_ldw, ldc0 + 32 * (h + 1 if h > 0 else 2))
                if h == 0:
                    k.wait("tensor", s_vc, v_ones)
                    k.wait("vector", s_ldw, ldtb)
                k.wait("tensor", s_sc, fin_sc[h])
                k.wait("tensor", s_vc, fin_vc[h])
                mmq = None
                for qg2 in range(2):
                    for c in range(12):
                        _mm(k, pb_[qg2][:, :], wqbb[h % 2][:, c * 192: c * 192 + 128],
                            sb_qcT2[:, c * 1024 + qg2 * 512: c * 1024 + (qg2 + 1) * 512],
                            c == 0, c == 11)
                        mmq = _mm(k, pb_[2 + qg2][:, :], wqbb[h % 2][:, c * 192 + 128: c * 192 + 256],
                                  sb_qcT2[:, c * 1024 + qg2 * 512: c * 1024 + (qg2 + 1) * 512],
                                  c == 0, c == 11,
                                  sem=(s_mm if (qg2 == 1 and c == 11) else None))
                k.wait("scalar", s_mm, mmq)
                qn = None
                for qg2 in range(2):
                    qn = _op(k, "scalar", lambda e, qg2=qg2: e.copy(
                        sb_qnT[:, qg2 * 512:(qg2 + 1) * 512], pb_[qg2][:, :]),
                        sem=(s_sc if qg2 == 1 else None))
                hp_qn[h] = qn
                k.wait("vector", s_mm, mmq)
                for qg2 in range(2):
                    _op(k, "vector", lambda e, qg2=qg2: e.tensor_scalar_mul(
                        t_qro[0:32, qg2 * 512:(qg2 + 1) * 512], pb_[2 + qg2][32:64, :], -1.0))
                    _op(k, "vector", lambda e, qg2=qg2: e.tensor_copy(
                        t_qro[32:64, qg2 * 512:(qg2 + 1) * 512], pb_[2 + qg2][0:32, :]))
                    _op(k, "vector", lambda e, qg2=qg2: e.tensor_copy(
                        t_qrb[:, qg2 * 512:(qg2 + 1) * 512], pb_[2 + qg2][0:64, :]))
                _op(k, "vector", lambda e: e.tensor_mul(t_qr2[:, :], t_qrb[:, :], sb_cos2[:, 0:1024]))
                _op(k, "vector", lambda e: e.tensor_mul(t_qr3[:, :], t_qro[:, :], sb_sin2[:, 0:1024]))
                qr = _op(k, "vector", lambda e: e.tensor_add(sb_qrT[0:64, :], t_qr2[:, :], t_qr3[:, :]), sem=s_vc)
                hp_qr[h] = qr
                mmk_ev = {}
                for kg in range(8):
                    if kg >= 2:
                        k.wait("tensor", s_sc2, mmk_ev[kg - 2])
                    mmk = None
                    for cc in range(4):
                        mmk = _mm(k, pb_[6 + kg % 2][:, :], wkvbb[h % 2][:, cc * 256: cc * 256 + 128],
                                  sb_kcT2[:, cc * S + kg * 512: cc * S + (kg + 1) * 512],
                                  cc == 0, cc == 3, sem=(s_mm if cc == 3 else None))
                    k.wait("scalar", s_mm, mmk)
                    mmk_ev[kg] = _op(k, "scalar", lambda e, kg=kg: e.copy(
                        sb_knT[:, kg * 512:(kg + 1) * 512], pb_[6 + kg % 2][:, :]), sem=s_sc2)
                knT_done = mmk_ev[7]
                k.wait("tensor", s_sc, hp_qn[h])
                vev = {}
                for kt4 in range(8):
                    if kt4 >= 2:
                        k.wait("tensor", s_sc3, vev[kt4 - 2])
                    mmv = None
                    for j4 in range(4):
                        kt = kt4 * 4 + j4
                        for cc in range(4):
                            mmv = _mm(k, pb_[kt4 % 2][:, j4 * 128:(j4 + 1) * 128],
                                      sb_kcT2[:, cc * S + kt * 128: cc * S + (kt + 1) * 128],
                                      wkvbb[h % 2][:, cc * 256 + 128:(cc + 1) * 256],
                                      cc == 0, cc == 3,
                                      sem=(s_mm if (j4 == 3 and cc == 3) else None))
                    k.wait("scalar", s_mm, mmv)
                    ev = _op(k, "scalar", lambda e, kt4=kt4: e.copy(
                        sb_vaug[:, kt4 * 512:(kt4 + 1) * 512], pb_[kt4 % 2][:, :]),
                        sem=s_sc3)
                    vev[kt4] = ev
                hp_wfree[h] = s_mm.n
                if h + 2 < NH:
                    k.wait("gpsimd", s_mm, hp_wfree[h])
                    c_load(h + 2)
                vaug_done = vev[7]
                k.wait("tensor", s_sc2, knT_done)
                k.wait("tensor", s_sc3, vaug_done)
                k.wait("tensor", s_vc, hp_qr[h])
                locfin = [fin_sc[h], fin_vc[h]]
                for qg2 in range(2):
                    exp_sc = {}
                    att_mm = {}
                    den_mm = {}
                    eA_vc = {}; eB_vc = {}; eG_vc = {}
                    def emit_attnv(ktl, qg2=qg2, locfin=locfin, att_mm=att_mm):
                        if ktl == 0:
                            k.wait("tensor", s_sc, locfin[0])
                            k.wait("tensor", s_vc, locfin[1])
                        k.wait("tensor", s_sc, exp_sc[ktl])
                        mma = _mm(k, pb_[2 + qg2][:, :],
                                  sb_vaug[:, ktl * 128:(ktl + 1) * 128],
                                  sb_exp[:, (ktl % 4) * 512:((ktl % 4) + 1) * 512],
                                  ktl == 0, ktl == NT - 1, sem=s_mm2)
                        att_mm[ktl] = mma
                        att_hist.append(mma)
                    def emit_den(g, qg2=qg2, locfin=locfin, den_mm=den_mm, eG_vc=eG_vc):
                        if g == 0:
                            k.wait("tensor", s_vc, locfin[1])
                        k.wait("tensor", s_vc, eG_vc[g])
                        den_mm[g] = _mm(k, pb_[4 + qg2][:, :], sb_one2[:, :],
                                        sb_eG[g % 2][:, :], g == 0, g == 7, sem=s_mm2)
                    for kt in range(NT):
                        g, j4 = kt // 4, kt % 4
                        sbank = (0, 1, 6, 7)[j4]
                        if len(exp_hist) >= 4:
                            k.wait("tensor", s_sc, exp_hist[-4])
                        _mm(k, pb_[sbank][:, :], sb_knT[:, kt * 128:(kt + 1) * 128],
                            sb_qnT[:, qg2 * 512:(qg2 + 1) * 512], True, False)
                        mms = _mm(k, pb_[sbank][:, :], sb_krT2[:, kt * 128:(kt + 1) * 128],
                                  sb_qrT[0:128, qg2 * 512:(qg2 + 1) * 512], False, True, sem=s_mm)
                        k.wait("scalar", s_mm, mms)
                        if len(att_hist) >= 2:
                            k.wait("scalar", s_mm2, att_hist[-2])
                        if g >= 1:
                            k.wait("scalar", s_vc, eA_vc[g - 1] if j4 < 2 else eB_vc[g - 1])
                        exp_sc[kt] = _op(k, "scalar", lambda e, kt=kt, sbank=sbank: e.activation(
                            sb_exp[:, (kt % 4) * 512:((kt % 4) + 1) * 512],
                            pb_[sbank][:, :], AF.Exp, scale=SCALE), sem=s_sc)
                        exp_hist.append(exp_sc[kt])
                        if j4 == 1:
                            k.wait("vector", s_sc, exp_sc[kt])
                            eA_vc[g] = _op(k, "vector", lambda e: e.tensor_add(
                                sb_eA[:, :], sb_exp[:, 0:512], sb_exp[:, 512:1024]), sem=s_vc)
                        if j4 == 3:
                            k.wait("vector", s_sc, exp_sc[kt])
                            eB_vc[g] = _op(k, "vector", lambda e: e.tensor_add(
                                sb_eB[:, :], sb_exp[:, 1024:1536], sb_exp[:, 1536:2048]), sem=s_vc)
                            k.wait("vector", s_vc, eB_vc[g])
                            if g >= 2:
                                k.wait("vector", s_mm2, den_mm[g - 2])
                            eG_vc[g] = _op(k, "vector", lambda e, g=g: e.tensor_add(
                                sb_eG[g % 2][:, :], sb_eA[:, :], sb_eB[:, :]), sem=s_vc)
                        if kt >= 2:
                            emit_attnv(kt - 2)
                        if j4 == 0 and g >= 2:
                            emit_den(g - 2)
                    emit_attnv(NT - 2)
                    emit_attnv(NT - 1)
                    emit_den(6)
                    emit_den(7)
                    k.wait("vector", s_mm2, att_mm[NT - 1])
                    k.wait("vector", s_mm2, den_mm[7])
                    k.wait("vector", s_vc, locfin[1])
                    v_ = _op(k, "vector", lambda e, qg2=qg2: e.reciprocal(
                        t_bc[:, :], pb_[4 + qg2][:, :]), sem=s_vc)
                    k.wait("vector", s_vc, v_)
                    fv = _op(k, "vector", lambda e, h=h, qg2=qg2: e.tensor_mul(
                        sb_aoT[:, h * 1024 + qg2 * 512: h * 1024 + (qg2 + 1) * 512],
                        pb_[2 + qg2][:, :], t_bc[:, :]), sem=s_vc)
                    locfin[0] = s_sc.n
                    locfin[1] = s_vc.n
                    if qg2 == 1:
                        fin_sc[h + 1] = s_sc.n
                        fin_vc[h + 1] = s_vc.n
            ppc.close()
            k.wait("gpsimd", s_sc, s_sc.n)
            k.wait("gpsimd", s_vc, s_vc.n)
            csub.close()

            if "D" in PHASES:
                pd_ = ExitStack(); pc.enter_context(pd_)
                wob = [sb("wob%d" % i, [128, 32 * 512], BF16, pd_) for i in range(2)]
                xbf = [sb("xbf%d" % i, [128, 512], F32, pd_) for i in range(4)]
                gbf2 = [sb("gbf2%d" % i, [128, 512], BF16, pd_) for i in range(4)]
                obf = [sb("obf%d" % i, [128, 512], F32, pd_) for i in range(2)]
                obm = sb("obm", [128, 512], F32, pd_)
                ps_w = [psum("ps_w%d" % i, [128, 512], pd_) for i in range(2)]

                ldd0 = s_ldw.n
                def d_wo_load(dg):
                    _dma(k, "gpsimd", wob[dg % 2][:, :].rearrange("p (a b) -> p a b", a=32),
                         d_wo[dg], sem=s_ldw)
                d_wo_load(0)
                d_wo_load(1)
                k.wait("tensor", s_vc, fin_vc[NH])
                ldx0 = s_ldx.n
                def d_xg_load(j, qt, dg):
                    _dma(k, "gpsimd", xbf[j % 4][:, :],
                         d_x[qt * 128:(qt + 1) * 128, dg * 512:(dg + 1) * 512], sem=s_ldx)
                    _dma(k, "gpsimd", gbf2[j % 4][:, :],
                         d_scr_g[qt * 128:(qt + 1) * 128, dg * 512:(dg + 1) * 512], sem=s_ldx)
                for j0 in range(4):
                    d_xg_load(j0, j0, 0)
                jglob = [4]
                wo_mm = {}
                ev_vc = {}
                gst_y0 = s_gst.n
                for dg in range(8):
                    k.wait("tensor", s_ldw, ldd0 + 16 * (dg + 1 if dg > 0 else 2))
                    for qt in range(8):
                        j = dg * 8 + qt
                        if j >= 2:
                            k.wait("tensor", s_vc, ev_vc[j - 2])
                        mmw = None
                        for hh in range(32):
                            mmw = _mm(k, ps_w[j % 2][:, :],
                                      sb_aoT[:, hh * 1024 + qt * 128: hh * 1024 + (qt + 1) * 128],
                                      wob[dg % 2][:, hh * 512:(hh + 1) * 512],
                                      hh == 0, hh == 31, sem=(s_mm if hh == 31 else None))
                        wo_mm[j] = mmw
                        k.wait("vector", s_mm, mmw)
                        k.wait("vector", s_ldx, ldx0 + 32 * (j + 1))
                        _op(k, "vector", lambda e, j=j: e.tensor_mul(
                            obm[:, :], ps_w[j % 2][:, :], gbf2[j % 4][:, :]))
                        if j >= 2:
                            k.wait("vector", s_gst, gst_y0 + 16 * (j - 1))
                        ev_vc[j] = _op(k, "vector", lambda e, j=j: e.tensor_add(
                            obf[j % 2][:, :], obm[:, :], xbf[j % 4][:, :]), sem=s_vc)
                        k.wait("gpsimd", s_vc, ev_vc[j])
                        _dma(k, "gpsimd", d_y[qt * 128:(qt + 1) * 128, dg * 512:(dg + 1) * 512],
                             obf[j % 2][:, :], sem=s_gst)
                        if jglob[0] < 64:
                            nj = jglob[0]
                            nqt, ndg = nj % 8, nj // 8
                            if nj - 4 >= 0 and (nj - 4) in ev_vc:
                                k.wait("gpsimd", s_vc, ev_vc[nj - 4])
                            d_xg_load(nj, nqt, ndg)
                            jglob[0] += 1
                    if dg + 2 < 8:
                        k.wait("gpsimd", s_mm, wo_mm[dg * 8 + 7])
                        d_wo_load(dg + 2)
                k.wait("gpsimd", s_gst, s_gst.n)
            pc.close()

        k.wait("sync", s_st, s_st.n)
        k.wait("gpsimd", s_gst, s_gst.n)

        k.run(es.enter_context(nc.Block()))
    return k


def _i8view(a):
    assert a.dtype == np.int32
    return np.ascontiguousarray(a.view(np.int8).reshape(*a.shape, 4)[..., 0])


def _host_prep(inputs):
    f32 = np.float32
    w_qa = np.ascontiguousarray(inputs["L0_attn_w_qa"], dtype=f32)
    w_qb = np.ascontiguousarray(inputs["L0_attn_w_qb"], dtype=f32)
    w_kva = np.ascontiguousarray(inputs["L0_attn_w_kva"], dtype=f32)
    w_kvb = np.ascontiguousarray(inputs["L0_attn_w_kvb"], dtype=f32)
    w_o = np.ascontiguousarray(inputs["L0_attn_w_o"], dtype=f32)
    gate_w = np.ascontiguousarray(inputs["L0_ffn_gate_w"], dtype=np.int32)
    up_w = np.ascontiguousarray(inputs["L0_ffn_up_w"], dtype=np.int32)
    down_w = np.ascontiguousarray(inputs["L0_ffn_down_w"], dtype=np.int32)
    shared = {
        "ident": np.eye(128, dtype=ml_dtypes.bfloat16),
        "wqa_t": np.ascontiguousarray(w_qa.reshape(32, 128, 3, 512).transpose(2, 1, 0, 3)).reshape(3, 128, 32 * 512),
        "wkva_t": np.ascontiguousarray(w_kva.reshape(32, 128, 576).transpose(1, 0, 2)),
        "wqb_t": np.ascontiguousarray(w_qb.reshape(12, 128, 32, 192).transpose(1, 2, 0, 3)),
        "wkvb_t": np.ascontiguousarray(w_kvb.reshape(4, 128, 32, 256).transpose(1, 2, 0, 3)),
        "gate_t": _i8view(np.ascontiguousarray(gate_w.reshape(NF, 128, 32, 128).transpose(0, 3, 2, 1)).reshape(NF, 128, 4096)),
        "up_t": _i8view(np.ascontiguousarray(up_w.reshape(NF, 128, 32, 128).transpose(0, 3, 2, 1)).reshape(NF, 128, 4096)),
        "down_t": _i8view(np.ascontiguousarray(down_w.reshape(8, 512, NF, 128).transpose(0, 3, 2, 1)).reshape(8, 128, NF * 512)),
        "wo_t": np.ascontiguousarray(w_o.reshape(32, 128, 8, 512).transpose(2, 1, 0, 3)),
        "gqa_t": np.ascontiguousarray(np.asarray(inputs["L0_attn_g_qa"], dtype=f32).reshape(12, 128).T),
        "gkva_t": np.ascontiguousarray(np.asarray(inputs["L0_attn_g_kva"], dtype=f32).reshape(4, 128).T),
        "dns": np.ascontiguousarray(np.asarray(inputs["L0_ffn_down_s"], dtype=f32).reshape(1, H)),
        "gs_t": np.ascontiguousarray(np.asarray(inputs["L0_ffn_gate_s"], dtype=f32).reshape(NF, 128).T),
        "us_t": np.ascontiguousarray(np.asarray(inputs["L0_ffn_up_s"], dtype=f32).reshape(NF, 128).T),
    }
    x = inputs["L0_attn_hidden_states"]
    in_maps = []
    for c in range(8):
        beta, sig = c // 4, c % 4
        m = dict(shared)
        m["x"] = np.ascontiguousarray(np.roll(np.asarray(x[beta], dtype=f32), -sig * 1024, axis=0))
        m["qbase"] = np.array([[sig * 1024.0]], dtype=f32)
        in_maps.append(m)
    return in_maps


def kernel(**inputs):
    in_maps = _host_prep(inputs)
    nc = bass.Bass("TRN2", target_bir_lowering=False, debug=False)
    build(nc)
    res = run_bass_kernel_spmd(nc, in_maps, core_ids=list(range(8)))
    out = np.empty((B, S, H), dtype=np.float32)
    for c in range(8):
        beta, sig = c // 4, c % 4
        out[beta, sig * 1024:(sig + 1) * 1024] = res.results[c]["y"]
    return out
